# version 22
# baseline (speedup 1.0000x reference)
import numpy as np
import contextlib
import concourse.bass as bass
import concourse.mybir as mybir
from concourse import library_config
from concourse.bass_isa import ReduceOp
from concourse.bass_utils import run_bass_kernel_spmd

F32 = mybir.dt.float32
BF16 = mybir.dt.bfloat16
FP8 = mybir.dt.float8e4
I32 = mybir.dt.int32
AF = mybir.ActivationFunctionType
ALU = mybir.AluOpType
AX = mybir.AxisListType

M32 = 12582912.0
C52 = 6755399441055744.0
EPS = 1e-5
D = 1024
HID = 4096
KB = 1024


TP_EXCLUSIVE = True


def build(NT=4096, NSEG=4):
    GT = NT // 128
    G = GT // NSEG
    assert G * NSEG * 128 == NT

    nc = bass.Bass("TRN2", target_bir_lowering=False)

    x_h = nc.dram_tensor("x", [NT, D], F32, kind="ExternalInput").ap()
    wqkv_h = nc.dram_tensor("w_qkv", [D, 3 * D], F32, kind="ExternalInput").ap()
    wo_h = nc.dram_tensor("w_o", [D, D], F32, kind="ExternalInput").ap()
    w1_h = nc.dram_tensor("w1", [HID, D], F32, kind="ExternalInput").ap()
    b1_h = nc.dram_tensor("b1", [1, HID], F32, kind="ExternalInput").ap()
    w2_h = nc.dram_tensor("w2", [D, HID], F32, kind="ExternalInput").ap()
    b2_h = nc.dram_tensor("b2", [1, D], F32, kind="ExternalInput").ap()
    gamma_h = nc.dram_tensor("gamma", [1, D], F32, kind="ExternalInput").ap()
    beta_h = nc.dram_tensor("beta", [1, D], F32, kind="ExternalInput").ap()
    out_h = nc.dram_tensor("out", [NT, D], F32, kind="ExternalOutput").ap()
    qkvs_h = nc.dram_tensor("qkvs", [D, 3 * D], BF16, kind="Internal").ap()
    wos_h = nc.dram_tensor("wos", [D, D], BF16, kind="Internal").ap()
    w1q_h = nc.dram_tensor("w1qs", [128, 8, HID], FP8, kind="Internal").ap()
    w2q_h = nc.dram_tensor("w2qs", [128, 32, D], FP8, kind="Internal").ap()

    es = contextlib.ExitStack()
    SB = lambda n, sh, dt=F32: es.enter_context(nc.sbuf_tensor(n, sh, dt))
    PS = lambda n, sh: es.enter_context(nc.psum_tensor(n, sh, F32))

    x2 = SB("x2", [128, G, D])
    gb = SB("gb", [128, D])
    bbt = SB("bbt", [128, D])
    mask4 = SB("mask4", [128, 4, 128], BF16)
    identf = SB("identf", [128, 128], F32)
    identb = SB("identb", [128, 128], BF16)
    b1r = SB("b1r", [1, HID], BF16)
    b2r = SB("b2r", [1, D], BF16)
    s1rowr = SB("s1rowr", [1, 2, 128], BF16)
    s2rowr = SB("s2rowr", [1, 2, 128], BF16)
    S_sb = SB("S_sb", [128, 8, 64], BF16)
    S_cr = SB("S_cr", [128, 512])
    mw = SB("mw", [128, 1])
    epsv = SB("epsv", [128, 1])
    swc = SB("swc", [128, 8])
    scal = SB("scal", [128, 4, 24])
    (SL_SD1, SL_R1, SL_B1, SL_SD2, SL_R2, SL_B2,
     SL_QM1, SL_S1, SL_DQ1, SL_SS1, SL_QM2, SL_S2, SL_DQ2, SL_SS2,
     SL_SD5, SL_R5, SL_B5, SL_T0, SL_T1, SL_T2, SL_T3) = range(21)
    mv12 = SB("mv12", [128, 4, 4])
    bns = SB("bns", [128, 12])
    khs = SB("khs", [128, 4, 64])
    mvf = SB("mvf", [128, 4, 4])
    mv5 = SB("mv5", [128, 4, 2])
    iid = SB("iid", [128, 128], I32)
    pw8t = SB("pw8t", [128, 4096], FP8)
    onesr = SB("onesr", [1, 128])
    onesc = SB("onesc", [128, 1])

    _wcm = [None, None]
    _wn = [0]

    def open_weights(kind):
        for i_ in (1, 0):
            if _wcm[i_] is not None:
                _wcm[i_].__exit__(None, None, None)
                _wcm[i_] = None
        _wn[0] += 1
        if kind == "attn":
            c0 = nc.sbuf_tensor(f"wqkv_t{_wn[0]}", [128, 8, 3072], BF16)
            c1 = nc.sbuf_tensor(f"wo_t{_wn[0]}", [128, 8, 1024], BF16)
        else:
            c0 = nc.sbuf_tensor(f"w1q_t{_wn[0]}", [128, 8, HID], FP8)
            c1 = nc.sbuf_tensor(f"w2q_t{_wn[0]}", [128, 32, D], FP8)
        _wcm[0], _wcm[1] = c0, c1
        return c0.__enter__().ap(), c1.__enter__().ap()

    af = SB("af32", [128, 13312])
    bf = SB("abf16", [128, 12288], BF16)

    def fv(e0, n, chunks=None):
        v = af[:, e0:e0 + n]
        return v if chunks is None else v.rearrange("p (c n) -> p c n", c=chunks)

    def bv(e0, n, chunks=None):
        v = bf[:, e0:e0 + n]
        return v if chunks is None else v.rearrange("p (c n) -> p c n", c=chunks)

    x_tm = fv(0, 2 * D, 2)
    k2sq = fv(2048, D)
    tmpa = fv(3072, D)
    h1_tm = bv(0, 2 * D, 2)
    h1_fm = bv(2048, 2048).rearrange("p (a c n) -> p a c n", a=2, c=8)
    qfm = bv(4096, 1024, 8)
    kln_tm = bv(5120, D)
    kln_fm = bv(6144, 2048).rearrange("p (a c n) -> p a c n", a=2, c=8)
    v_tm = bv(8192, D)
    scT = bv(9216, 2048).rearrange("p (e q n) -> p e q n", e=2, q=8)
    yfm = bv(11264, 1024, 8)
    h3r = fv(0, 2 * D, 2)
    h4r = fv(2048, 2 * D, 2)
    fwk = fv(4096, 2 * D, 2)
    g_sb = fv(6144, HID)
    y2 = fv(10240, 1024)
    ztr = fv(11264, 2 * D, 2)
    xq1_tm = bv(0, 2 * D, 2)
    xq1_fm = bv(2048, 2048).rearrange("p (a c n) -> p a c n", a=2, c=8)
    xq2_tm = bv(4096, HID)
    xq2_fm = bv(8192, 4096, 32)
    pw_f = fv(0, 4096)
    pw_f2 = fv(4096, 4096)
    pacc = fv(8192, 40)
    pw_b = bv(0, 4096)
    pw_t1 = bv(4096, 2048, 16)
    pw_t2 = bv(4096, 4096, 32)
    pw_81 = pw8t[:, 0:2048].rearrange("p (c n) -> p c n", c=16)
    pw_82 = pw8t[:, 0:4096].rearrange("p (c n) -> p c n", c=32)
    pw_t1b = bv(8192, 2048, 16)
    pw_bh = bv(8192, 4096)

    Bk = [PS(f"pb{i}", [128, 512]) for i in range(8)]

    s_g = [es.enter_context(nc.semaphore(f"s_g{i}")) for i in range(8)]
    s_tp = es.enter_context(nc.semaphore("s_tp"))
    s_pe = es.enter_context(nc.semaphore("s_pe"))
    s_dv = es.enter_context(nc.semaphore("s_dv"))
    s_ac = es.enter_context(nc.semaphore("s_ac"))
    s_gp = es.enter_context(nc.semaphore("s_gp"))

    blk = es.enter_context(nc.Block())
    PE, V, A, SY, GP = nc.tensor, nc.vector, nc.scalar, nc.sync, nc.gpsimd

    cnt = {"tp": 0, "pe": 0, "dv": 0, "ac": 0, "gp": 0}
    gslot = [0] * 8
    gring = [0]
    ms = {}

    def gd_dma(out, in_):
        i = gring[0] % 8
        gring[0] += 1
        GP.wait_ge(s_g[i], gslot[i])
        if TP_EXCLUSIVE:
            GP.wait_ge(s_tp, cnt["tp"])
        gslot[i] += 16
        GP.dma_start(out=out, in_=in_).then_inc(s_g[i], 16)
        return (i, gslot[i])

    def wait_gd(eng, h):
        eng.wait_ge(s_g[h[0]], h[1])

    def tp_dma(out, in_):
        if TP_EXCLUSIVE:
            for i in range(8):
                SY.wait_ge(s_g[i], gslot[i])
        SY.wait_ge(s_tp, cnt["tp"])
        cnt["tp"] += 16
        SY.dma_start_transpose(out=out, in_=in_).then_inc(s_tp, 16)

    def pe_inc():
        cnt["pe"] += 1
        PE.drain().then_inc(s_pe, 1)

    def pe_mark(inst):
        cnt["pe"] += 1
        inst.then_inc(s_pe, 1)

    def dv_inc():
        cnt["dv"] += 1
        _pend.clear()
        V.drain().then_inc(s_dv, 1)

    def ac_inc():
        cnt["ac"] += 1
        A.drain().then_inc(s_ac, 1)

    def ac_mark(inst):
        cnt["ac"] += 1
        inst.then_inc(s_ac, 1)

    _pend = set()

    def vdrain():
        V.drain()
        _pend.clear()

    def vo(tag, reads, emit):
        if _pend:
            vdrain()
        emit()
        _pend.add(tag)

    def vmemset(tag, ap, val):
        V.memset(ap, val)
        _pend.add(tag)

    GP.iota(out=iid[:], pattern=[[1, 128]], base=0, channel_multiplier=-1)
    GP.drain().then_inc(s_gp, 1); cnt["gp"] += 1

    wqkv, wo = open_weights("attn")
    hs = [gd_dma(gb[0:1, :], gamma_h),
          gd_dma(bbt[0:1, :], beta_h),
          gd_dma(b1r[:], b1_h),
          gd_dma(b2r[:], b2_h),
          gd_dma(wqkv, wqkv_h.rearrange("(c p) n -> p c n", p=128)),
          gd_dma(wo, wo_h.rearrange("(c p) n -> p c n", p=128))]
    for h_ in hs[4:]:
        wait_gd(GP, h_)
    gd_dma(qkvs_h.rearrange("(c p) n -> p c n", p=128), wqkv)
    gd_dma(wos_h.rearrange("(c p) n -> p c n", p=128), wo)

    V.wait_ge(s_gp, 1)
    vmemset("onesr", onesr[:], 1.0)
    vmemset("onesc", onesc[:], 1.0)
    dv_inc()
    PE.wait_ge(s_dv, cnt["dv"])
    for h_ in hs[0:2]:
        wait_gd(PE, h_)
    for n2 in range(2):
        PE.matmul(Bk[n2][:, :], lhsT=onesr[:], rhs=gb[0:1, n2 * 512:(n2 + 1) * 512],
                  start=True, stop=True, skip_group_check=True)
        PE.matmul(Bk[2 + n2][:, :], lhsT=onesr[:],
                  rhs=bbt[0:1, n2 * 512:(n2 + 1) * 512],
                  start=True, stop=True, skip_group_check=True)
    pe_inc()
    A.wait_ge(s_pe, cnt["pe"])
    A.copy(out=gb[:, 0:512], in_=Bk[0][:])
    A.copy(out=gb[:, 512:1024], in_=Bk[1][:])
    A.copy(out=bbt[:, 0:512], in_=Bk[2][:])
    A.copy(out=bbt[:, 512:1024], in_=Bk[3][:])
    ac_inc()
    vmemset("pilm", swc[:, 4:5], 0.75)
    vo("identf", ["iid"], lambda: V.tensor_scalar(
        out=identf[:], in0=iid[:], scalar1=0, scalar2=None, op0=ALU.is_equal))
    vo("identb", ["identf"], lambda: V.tensor_copy(out=identb[:], in_=identf[:]))
    for j in range(4):
        vo("mask4", ["iid"], lambda j=j: V.tensor_scalar(
            out=mask4[:, j, :], in0=iid[:], scalar1=0, scalar2=None, op0=ALU.is_ge))
    vo("pil", ["pilm"], lambda: V.tensor_scalar(
        out=swc[:, 4:5], in0=swc[:, 4:5], scalar1=C52, scalar2=C52,
        op0=ALU.add, op1=ALU.subtract))
    V.drain()
    vo("mw", ["pil"], lambda: V.tensor_scalar(
        out=mw[:], in0=swc[:, 4:5], scalar1=C52, scalar2=M32,
        op0=ALU.mult, op1=ALU.add))
    vmemset("epsm", epsv[:], EPS)
    vmemset("Ssbm", S_sb[:], 0)
    vmemset("Scrm", S_cr[:], 0)
    dv_inc(); ms["dv_identb"] = cnt["dv"]

    def prep_w(which):
        wsrc_t = (w1_h.rearrange("(a p) n -> p a n", p=128) if which == 1
                  else w2_h.rearrange("(a p) n -> p a n", p=128))
        nt = 32 if which == 1 else 8
        cols = D if which == 1 else HID
        per = 4 if which == 1 else 1
        wdst = w1q_h if which == 1 else w2q_h
        sw_s = swc[:, 0:1] if which == 1 else swc[:, 2:3]
        inv_s = swc[:, 1:2] if which == 1 else swc[:, 3:4]
        pw8 = pw_81 if which == 1 else pw_82
        nld = nt // per
        hs1 = []
        for i in range(nld):
            buf = pw_f if i % 2 == 0 else pw_f2
            src_ = (wsrc_t[:, i * per:(i + 1) * per, :] if per > 1
                    else wsrc_t[:, i, :])
            dstv = (buf.rearrange("p (a n) -> p a n", a=per) if per > 1 else buf)
            if i >= 2:
                GP.wait_ge(s_dv, ms[f"prep_pa_{which}_{i-2}"])
            hs1.append(gd_dma(dstv, src_))
            wait_gd(V, hs1[-1])
            vo(f"pa{i}", [], lambda i=i, dv=dstv: V.tensor_reduce(
                out=pacc[:, i:i + 1], in_=dv,
                axis=(AX.XY if per > 1 else AX.X), op=ALU.add,
                apply_absolute_value=True))
            dv_inc(); ms[f"prep_pa_{which}_{i}"] = cnt["dv"]
        vo("pt", [], lambda: V.tensor_reduce(
            out=pacc[:, 34:35], in_=pacc[:, 0:nld], axis=AX.X, op=ALU.add))
        dv_inc()
        PE.wait_ge(s_dv, cnt["dv"])
        PE.matmul(Bk[4][0:1, 0:1], lhsT=pacc[:, 34:35], rhs=onesc[:],
                  start=True, stop=True, skip_group_check=True)
        pe_inc()
        A.wait_ge(s_pe, cnt["pe"])
        A.copy(out=pacc[0:1, 38:39], in_=Bk[4][0:1, 0:1])
        ac_inc()
        PE.wait_ge(s_ac, cnt["ac"])
        PE.matmul(Bk[4][:, 0:1], lhsT=onesr[:], rhs=pacc[0:1, 38:39],
                  start=True, stop=True, skip_group_check=True)
        pe_inc()
        A.wait_ge(s_pe, cnt["pe"])
        A.copy(out=pacc[:, 35:36], in_=Bk[4][:, 0:1])
        ac_inc()
        V.wait_ge(s_ac, cnt["ac"])
        vo("wm", [], lambda: V.tensor_scalar(
            out=pacc[:, 36:37], in0=pacc[:, 35:36], scalar1=1.0 / (nt * 128 * cols),
            scalar2=EPS, op0=ALU.mult, op1=ALU.max))
        vo("sw", ["wm"], lambda: V.reciprocal(out=sw_s, in_=pacc[:, 36:37]))
        vo("s127", ["sw"], lambda: V.tensor_scalar(
            out=pacc[:, 37:38], in0=sw_s, scalar1=127.0, scalar2=None, op0=ALU.mult))
        vo("inv", ["s127"], lambda: V.reciprocal(out=inv_s, in_=pacc[:, 37:38]))
        dv_inc(); ms[f"prep_mean_{which}"] = cnt["dv"]
        def quant3(tag, fb, bb):
            vo(f"{tag}a", [], lambda b=fb: V.tensor_scalar(
                out=b, in0=b, scalar1=sw_s, scalar2=mw[:], op0=ALU.mult, op1=ALU.add))
            vo(f"{tag}b", [f"{tag}a"], lambda b=fb: V.tensor_scalar(
                out=b, in0=b, scalar1=mw[:], scalar2=None, op0=ALU.subtract))
            vo(f"{tag}c", [f"{tag}b"], lambda b=fb, o=bb: V.tensor_scalar(
                out=o, in0=b, scalar1=-1.0, scalar2=1.0, op0=ALU.max, op1=ALU.min))
        sth = {}
        if which == 1:
            fsl = ([pw_f[:, j * 1024:(j + 1) * 1024] for j in range(4)]
                   + [pw_f2[:, j * 1024:(j + 1) * 1024] for j in range(4)])
            tsl = ([pw_t1[:, j * 8:(j + 1) * 8, :] for j in range(2)]
                   + [pw_t1b[:, j * 8:(j + 1) * 8, :] for j in range(2)])
            ldh = {}
            for t in range(6):
                if t == 0:
                    GP.wait_ge(s_dv, ms[f"prep_mean_{which}"])
                ldh[t] = gd_dma(fsl[t % 8], wsrc_t[:, t, :])
            for t in range(32):
                s4, s8 = t % 4, t % 8
                wait_gd(V, ldh[t])
                quant3(f"q{which}_{t}", fsl[s8], fsl[s8])
                dv_inc(); ms[f"prep_q_{which}_{t}"] = cnt["dv"]
                PE.wait_ge(s_dv, cnt["dv"])
                for grp in range(2):
                    bank = 5 + grp
                    if t >= 1:
                        PE.wait_ge(s_ac, ms[f"prep_ev_{which}_{t-1}_{grp}"])
                    for q in range(4):
                        c = grp * 4 + q
                        mi = PE.transpose(Bk[bank][:, q * 128:(q + 1) * 128],
                                          in_=fsl[s8][:, c * 128:(c + 1) * 128],
                                          identity=identf[:])
                    pe_mark(mi); ms[f"prep_pt_{which}_{t}_{grp}"] = cnt["pe"]
                    A.wait_ge(s_pe, cnt["pe"])
                    if grp == 0 and t >= 4:
                        wait_gd(A, sth[t - 4])
                    ai = A.copy(out=tsl[s4][:, grp * 4:(grp + 1) * 4, :],
                                in_=Bk[bank][:].rearrange("p (a n) -> p a n", a=4))
                    ac_mark(ai); ms[f"prep_ev_{which}_{t}_{grp}"] = cnt["ac"]
                ms[f"prep_pt_{which}_{t}"] = ms[f"prep_pt_{which}_{t}_1"]
                if t + 6 < 32:
                    if t >= 2:
                        GP.wait_ge(s_pe, ms[f"prep_pt_{which}_{t-2}"])
                    ldh[t + 6] = gd_dma(fsl[(t + 6) % 8], wsrc_t[:, t + 6, :])
                GP.wait_ge(s_ac, ms[f"prep_ev_{which}_{t}_1"])
                sth[t] = gd_dma(wdst[:, :, t * 128:(t + 1) * 128], tsl[s4])
        else:
            fsl = [pw_f[:, 0:2048], pw_f[:, 2048:4096],
                   pw_f2[:, 0:2048], pw_f2[:, 2048:4096]]
            tsl = [pw_t2[:, 0:16, :], pw_t2[:, 16:32, :]]
            ldh = {}
            for ht in range(3):
                if ht == 0:
                    GP.wait_ge(s_dv, ms[f"prep_mean_{which}"])
                ldh[ht] = gd_dma(fsl[ht % 4],
                                 wsrc_t[:, ht // 2, (ht % 2) * 2048:(ht % 2) * 2048 + 2048])
            for ht in range(16):
                i_, hh = ht // 2, ht % 2
                wait_gd(V, ldh[ht])
                quant3(f"q{which}_{ht}", fsl[ht % 4], fsl[ht % 4])
                dv_inc(); ms[f"prep_q_{which}_{ht}"] = cnt["dv"]
                PE.wait_ge(s_dv, cnt["dv"])
                for grp in range(4):
                    bank = 5 + (grp % 2)
                    pt, pg = (ht, grp - 2) if grp >= 2 else (ht - 1, grp + 2)
                    if pt >= 0:
                        PE.wait_ge(s_ac, ms[f"prep_ev_{which}_{pt}_{pg}"])
                    elif grp < 2:
                        PE.wait_ge(s_ac, ms[f"prep_ev_1_31_{grp}"])
                    for q in range(4):
                        c = grp * 4 + q
                        mi = PE.transpose(Bk[bank][:, q * 128:(q + 1) * 128],
                                          in_=fsl[ht % 4][:, c * 128:(c + 1) * 128],
                                          identity=identf[:])
                    pe_mark(mi); ms[f"prep_pt2_{ht}_{grp}"] = cnt["pe"]
                    A.wait_ge(s_pe, cnt["pe"])
                    if grp == 0 and ht >= 2:
                        wait_gd(A, sth[ht - 2])
                    ai = A.copy(out=tsl[ht % 2][:, grp * 4:(grp + 1) * 4, :],
                                in_=Bk[bank][:].rearrange("p (a n) -> p a n", a=4))
                    ac_mark(ai); ms[f"prep_ev_{which}_{ht}_{grp}"] = cnt["ac"]
                ms[f"prep_pt_{which}_{ht}"] = ms[f"prep_pt2_{ht}_3"]
                if ht + 3 < 16:
                    if ht >= 1:
                        GP.wait_ge(s_pe, ms[f"prep_pt_{which}_{ht-1}"])
                    nh = ht + 3
                    ldh[nh] = gd_dma(
                        fsl[nh % 4],
                        wsrc_t[:, nh // 2, (nh % 2) * 2048:(nh % 2) * 2048 + 2048])
                GP.wait_ge(s_ac, ms[f"prep_ev_{which}_{ht}_3"])
                sth[ht] = gd_dma(
                    wdst[:, hh * 16:(hh + 1) * 16, i_ * 128:(i_ + 1) * 128],
                    tsl[ht % 2])
    prep_w(1)
    prep_w(2)
    ms["prep_dv"] = cnt["dv"]

    for seg in range(NSEG):
        g0 = seg * G
        if seg > 0:
            GP.wait_ge(s_pe, ms[f"pe_ffn_done_{seg-1}"])
            wqkv, wo = open_weights("attn")
            ms[f"attnw_{seg}"] = [
                gd_dma(wqkv, qkvs_h.rearrange("(c p) n -> p c n", p=128)),
                gd_dma(wo, wos_h.rearrange("(c p) n -> p c n", p=128))]
        else:
            ms[f"attnw_{seg}"] = []

        def a_load(gg):
            xb = x_tm[:, gg % 2, :]
            if gg >= 2:
                GP.wait_ge(s_dv, ms[f"dv_x2_{seg}_{gg-2}"])
            elif seg > 0:
                GP.wait_ge(s_dv, ms[f"dv_out_{seg-1}_{G-1}"])
            ms[f"gd_x_{seg}_{gg}"] = gd_dma(xb, x_h[(g0 + gg) * 128:(g0 + gg) * 128 + 128, :])

        def a_front(gg):
            r = gg % 4
            xb = x_tm[:, gg % 2, :]
            wait_gd(V, ms[f"gd_x_{seg}_{gg}"])
            vo("bnsA", [], lambda: V.bn_stats(out=bns[:, 0:6], in_=xb[:, 0:512]))
            vo("bnsB", [], lambda: V.bn_stats(out=bns[:, 6:12], in_=xb[:, 512:1024]))
            vo("mv1", [], lambda: V.bn_aggr(out=mv12[:, r, 0:2], in_=bns[:]))
            dv_inc(); ms[f"dv_ln1s_{seg}_{gg}"] = cnt["dv"]
            A.wait_ge(s_dv, ms[f"dv_ln1s_{seg}_{gg}"])
            ai = A.activation(out=scal[:, r, SL_SD1:SL_SD1 + 1], in_=mv12[:, r, 1:2],
                              func=AF.Sqrt, bias=epsv[:])
            ac_mark(ai); ms[f"ac_sd1_{seg}_{gg}"] = cnt["ac"]
            V.wait_ge(s_ac, ms[f"ac_sd1_{seg}_{gg}"])
            vo("r1", [], lambda: V.reciprocal(
                out=scal[:, r, SL_R1:SL_R1 + 1], in_=scal[:, r, SL_SD1:SL_SD1 + 1]))
            vo("b1s", [], lambda: V.tensor_scalar(
                out=scal[:, r, SL_B1:SL_B1 + 1], in0=mv12[:, r, 0:1],
                scalar1=scal[:, r, SL_R1:SL_R1 + 1], scalar2=-1.0,
                op0=ALU.mult, op1=ALU.mult))
            dv_inc(); ms[f"dv_ln1r_{seg}_{gg}"] = cnt["dv"]
            A.wait_ge(s_dv, ms[f"dv_ln1r_{seg}_{gg}"])
            if gg >= 2:
                A.wait_ge(s_tp, ms[f"tp_h1_{seg}_{gg-2}"])
            elif seg > 0:
                A.wait_ge(s_tp, ms[f"tp_xq2_{seg-1}_{G-1}_1"])
            hb = h1_tm[:, gg % 2, :]
            ai = A.activation(out=hb, in_=xb, func=AF.Identity,
                              scale=scal[:, r, SL_R1:SL_R1 + 1],
                              bias=scal[:, r, SL_B1:SL_B1 + 1])
            ac_mark(ai); ms[f"ac_h1_{seg}_{gg}"] = cnt["ac"]

        def a_front_tp(gg):
            SY.wait_ge(s_ac, ms[f"ac_h1_{seg}_{gg}"])
            if gg >= 2:
                SY.wait_ge(s_pe, ms[f"pe_v1_{seg}_{gg-2}"])
            tp_dma(h1_fm[:, gg % 2, :, :], h1_tm[:, gg % 2, :])
            ms[f"tp_h1_{seg}_{gg}"] = cnt["tp"]

        def a_qk(gg):
            hf = h1_fm[:, gg % 2, :, :]
            if gg == 0:
                for h_ in ms[f"attnw_{seg}"]:
                    wait_gd(PE, h_)
            PE.wait_ge(s_tp, ms[f"tp_h1_{seg}_{gg}"])
            if gg >= 1:
                PE.wait_ge(s_ac, ms[f"ac_ve1_{seg}_{gg-1}"])
                PE.wait_ge(s_pe, ms[f"pe_y2_{seg}_{gg-1}"])
            for m in range(8):
                pq = Bk[m // 4][:, (m % 4) * 128:(m % 4) * 128 + 128]
                for kk in range(8):
                    mi = PE.matmul(pq, lhsT=wqkv[:, kk, m * 128:(m + 1) * 128],
                                   rhs=hf[:, kk, :], start=(kk == 0), stop=(kk == 7))
            pe_mark(mi); ms[f"pe_q_{seg}_{gg}"] = cnt["pe"]
            if gg >= 1:
                PE.wait_ge(s_ac, ms[f"ac_ye_{seg}_{gg-1}"])
            for kk in range(8):
                for n2 in range(2):
                    mi = PE.matmul(Bk[2 + n2][:, :], lhsT=hf[:, kk, :],
                                   rhs=wqkv[:, kk, 1024 + n2 * 512:1536 + n2 * 512],
                                   start=(kk == 0), stop=(kk == 7))
            pe_mark(mi); ms[f"pe_k_{seg}_{gg}"] = cnt["pe"]
            A.wait_ge(s_pe, ms[f"pe_q_{seg}_{gg}"])
            A.copy(out=qfm[:, 0:4, :], in_=Bk[0][:].rearrange("p (a n) -> p a n", a=4))
            ai = A.copy(out=qfm[:, 4:8, :], in_=Bk[1][:].rearrange("p (a n) -> p a n", a=4))
            ac_mark(ai); ms[f"ac_qe_{seg}_{gg}"] = cnt["ac"]
            A.wait_ge(s_pe, ms[f"pe_k_{seg}_{gg}"])
            A.activation(out=k2sq[:, 0:512], in_=Bk[2][:], func=AF.Square)
            ai = A.activation(out=k2sq[:, 512:1024], in_=Bk[3][:], func=AF.Square)
            ac_mark(ai); ms[f"ac_k2_{seg}_{gg}"] = cnt["ac"]

        def a_v(gg):
            hf = h1_fm[:, gg % 2, :, :]
            PE.wait_ge(s_ac, ms[f"ac_qe_{seg}_{gg}"])
            for vh in range(2):
                for kk in range(8):
                    mi = PE.matmul(Bk[vh][:, :], lhsT=hf[:, kk, :],
                                   rhs=wqkv[:, kk, 2048 + vh * 512:2560 + vh * 512],
                                   start=(kk == 0), stop=(kk == 7))
                pe_mark(mi); ms[f"pe_v{vh}_{seg}_{gg}"] = cnt["pe"]

        def a_kchain(gg):
            r = gg % 4
            V.wait_ge(s_pe, ms[f"pe_k_{seg}_{gg}"])
            V.wait_ge(s_ac, ms[f"ac_k2_{seg}_{gg}"])
            kp2 = Bk[2][:].rearrange("p (h e) -> p h e", e=64)
            kp3 = Bk[3][:].rearrange("p (h e) -> p h e", e=64)
            vo("ks1", [], lambda: V.tensor_reduce(
                out=khs[:, r, 0:8], in_=kp2, axis=AX.X, op=ALU.add))
            vo("ks2", [], lambda: V.tensor_reduce(
                out=khs[:, r, 8:16], in_=kp3, axis=AX.X, op=ALU.add))
            vo("ks3", [], lambda: V.tensor_reduce(
                out=khs[:, r, 16:32],
                in_=k2sq.rearrange("p (h e) -> p h e", e=64), axis=AX.X, op=ALU.add))
            vo("kmu", [], lambda: V.tensor_scalar(
                out=khs[:, r, 32:48], in0=khs[:, r, 0:16], scalar1=1.0 / 64,
                scalar2=None, op0=ALU.mult))
            vo("ke2", [], lambda: V.tensor_scalar(
                out=khs[:, r, 48:64], in0=khs[:, r, 16:32], scalar1=1.0 / 64,
                scalar2=None, op0=ALU.mult))
            vo("kmsq", [], lambda: V.tensor_tensor(
                out=khs[:, r, 0:16], in0=khs[:, r, 32:48], in1=khs[:, r, 32:48],
                op=ALU.mult))
            vo("kvar", [], lambda: V.tensor_tensor(
                out=khs[:, r, 16:32], in0=khs[:, r, 48:64], in1=khs[:, r, 0:16],
                op=ALU.subtract))
            dv_inc(); ms[f"dv_kvar_{seg}_{gg}"] = cnt["dv"]
            A.wait_ge(s_dv, ms[f"dv_kvar_{seg}_{gg}"])
            ai = A.activation(out=khs[:, r, 48:64], in_=khs[:, r, 16:32],
                              func=AF.Sqrt, bias=epsv[:])
            ac_mark(ai); ms[f"ac_ksd_{seg}_{gg}"] = cnt["ac"]
            V.wait_ge(s_ac, ms[f"ac_ksd_{seg}_{gg}"])
            vo("krstd", [], lambda: V.reciprocal(
                out=khs[:, r, 16:32], in_=khs[:, r, 48:64]))
            if gg >= 1:
                V.wait_ge(s_tp, ms[f"tp_k1_{seg}_{gg-1}"])
                V.wait_ge(s_pe, ms[f"pe_upd_{seg}_{gg-1}"])
            a_kapp_half(gg, 0)

        def a_kapp_half(gg, half_):
            r = gg % 4
            kp2 = Bk[2][:].rearrange("p (h e) -> p h e", e=64)
            kp3 = Bk[3][:].rearrange("p (h e) -> p h e", e=64)
            kf = kln_fm[:, gg % 2, :, :]
            bk_ = kp2 if half_ == 0 else kp3
            mu0 = 32 + half_ * 8
            r0 = 16 + half_ * 8
            vo("kcen", ["kmu"], lambda bk_=bk_, mu0=mu0, half_=half_:
               V.tensor_tensor(
                out=k2sq[:, half_ * 512:half_ * 512 + 512].rearrange(
                    "p (h e) -> p h e", e=64),
                in0=bk_,
                in1=khs[:, r, mu0:mu0 + 8].unsqueeze(-1).broadcast_to([128, 8, 64]),
                op=ALU.subtract))
            vo("kapp", ["kcen", "krstd"], lambda half_=half_, r0=r0:
               V.tensor_tensor(
                out=kln_tm[:, half_ * 512:half_ * 512 + 512].rearrange(
                    "p (h e) -> p h e", e=64),
                in0=k2sq[:, half_ * 512:half_ * 512 + 512].rearrange(
                    "p (h e) -> p h e", e=64),
                in1=khs[:, r, r0:r0 + 8].unsqueeze(-1).broadcast_to([128, 8, 64]),
                op=ALU.mult))
            dv_inc(); ms[f"dv_kln{half_}_{seg}_{gg}"] = cnt["dv"]
            SY.wait_ge(s_dv, cnt["dv"])
            if gg >= 2:
                SY.wait_ge(s_pe, ms[f"pe_sc2_{seg}_{gg-2}"])
            tp_dma(kf[:, half_ * 4:half_ * 4 + 4, :],
                   kln_tm[:, half_ * 512:half_ * 512 + 512])
            ms[f"tp_k{half_}_{seg}_{gg}"] = cnt["tp"]
            if half_ == 1:
                ms[f"dv_kln_{seg}_{gg}"] = ms[f"dv_kln1_{seg}_{gg}"]

        def a_vevac(gg):
            A.wait_ge(s_pe, ms[f"pe_v0_{seg}_{gg}"])
            if gg >= 1:
                A.wait_ge(s_pe, ms[f"pe_upd_{seg}_{gg-1}"])
            ai = A.copy(out=v_tm[:, 0:512], in_=Bk[0][:])
            ac_mark(ai); ms[f"ac_ve0_{seg}_{gg}"] = cnt["ac"]
            A.wait_ge(s_pe, ms[f"pe_v1_{seg}_{gg}"])
            ai = A.copy(out=v_tm[:, 512:1024], in_=Bk[1][:])
            ac_mark(ai); ms[f"ac_ve1_{seg}_{gg}"] = cnt["ac"]
            ms[f"ac_ve_{seg}_{gg}"] = ms[f"ac_ve1_{seg}_{gg}"]

        def a_sc(gg):
            kf = kln_fm[:, gg % 2, :, :]
            PE.wait_ge(s_ac, ms[f"ac_qe_{seg}_{gg}"])
            PE.wait_ge(s_tp, ms[f"tp_k0_{seg}_{gg}"])
            if gg >= 1:
                PE.wait_ge(s_ac, ms[f"ac_ln2a_{seg}_{gg-1}"])
            for h in range(8):
                par = h % 2
                mi = PE.matmul(Bk[4 + par][:, (h // 2) * 128:(h // 2) * 128 + 128],
                               lhsT=kf[par * 64:par * 64 + 64, h // 2, :],
                               rhs=qfm[par * 64:par * 64 + 64, h // 2, :],
                               start=True, stop=True, tile_position=(par * 64, 0),
                               skip_group_check=True)
            pe_mark(mi); ms[f"pe_sc1_{seg}_{gg}"] = cnt["pe"]
            V.wait_ge(s_pe, ms[f"pe_sc1_{seg}_{gg}"])
            if gg >= 1:
                V.wait_ge(s_pe, ms[f"pe_y2_{seg}_{gg-1}"])
            for par in range(2):
                vo("scm1", [], lambda par=par: V.tensor_tensor(
                    out=scT[:, par, 0:4, :],
                    in0=Bk[4 + par][:].rearrange("p (a n) -> p a n", a=4),
                    in1=mask4[:], op=ALU.mult))
            dv_inc(); ms[f"dv_mask1_{seg}_{gg}"] = cnt["dv"]
            PE.wait_ge(s_tp, ms[f"tp_k1_{seg}_{gg}"])
            PE.wait_ge(s_dv, ms[f"dv_mask1_{seg}_{gg}"])
            for h in range(8, 16):
                par = h % 2
                fr = ((h - 8) // 2) * 128
                mi = PE.matmul(Bk[4 + par][:, fr:fr + 128],
                               lhsT=kf[par * 64:par * 64 + 64, h // 2, :],
                               rhs=qfm[par * 64:par * 64 + 64, h // 2, :],
                               start=True, stop=True, tile_position=(par * 64, 0),
                               skip_group_check=True)
            pe_mark(mi); ms[f"pe_sc2_{seg}_{gg}"] = cnt["pe"]
            V.wait_ge(s_pe, ms[f"pe_sc2_{seg}_{gg}"])
            for par in range(2):
                vo("scm2", [], lambda par=par: V.tensor_tensor(
                    out=scT[:, par, 4:8, :],
                    in0=Bk[4 + par][:].rearrange("p (a n) -> p a n", a=4),
                    in1=mask4[:], op=ALU.mult))
            dv_inc(); ms[f"dv_mask2_{seg}_{gg}"] = cnt["dv"]

        def a_y(gg):
            PE.wait_ge(s_dv, ms[f"dv_mask1_{seg}_{gg}"])
            PE.wait_ge(s_dv, ms[f"dv_kln0_{seg}_{gg}"])
            PE.wait_ge(s_ac, ms[f"ac_ve_{seg}_{gg}"])
            if gg >= 1:
                PE.wait_ge(s_dv, ms[f"dv_Ssb_{seg}_{gg-1}"])
            for h in range(8):
                par = h % 2
                dst = Bk[2][par * 64:par * 64 + 64,
                            (h // 2) * 128:(h // 2) * 128 + 128]
                PE.matmul(dst, lhsT=S_sb[par * 64:par * 64 + 64, h // 2, :],
                          rhs=qfm[par * 64:par * 64 + 64, h // 2, :],
                          start=True, stop=False,
                          tile_position=(par * 64, par * 64), skip_group_check=True)
                mi = PE.matmul(dst, lhsT=v_tm[:, h * 64:(h + 1) * 64],
                               rhs=scT[:, par, h // 2, :],
                               start=False, stop=True, tile_position=(0, par * 64),
                               skip_group_check=True)
            pe_mark(mi); ms[f"pe_y1_{seg}_{gg}"] = cnt["pe"]
            PE.wait_ge(s_dv, ms[f"dv_mask2_{seg}_{gg}"])
            PE.wait_ge(s_dv, ms[f"dv_kln_{seg}_{gg}"])
            for h in range(8, 16):
                par = h % 2
                fr = ((h - 8) // 2) * 128
                dst = Bk[3][par * 64:par * 64 + 64, fr:fr + 128]
                PE.matmul(dst, lhsT=S_sb[par * 64:par * 64 + 64, h // 2, :],
                          rhs=qfm[par * 64:par * 64 + 64, h // 2, :],
                          start=True, stop=False,
                          tile_position=(par * 64, par * 64), skip_group_check=True)
                mi = PE.matmul(dst, lhsT=v_tm[:, h * 64:(h + 1) * 64],
                               rhs=scT[:, par, h // 2, :],
                               start=False, stop=True, tile_position=(0, par * 64),
                               skip_group_check=True)
            pe_mark(mi); ms[f"pe_y2_{seg}_{gg}"] = cnt["pe"]
            A.wait_ge(s_pe, ms[f"pe_y1_{seg}_{gg}"])
            A.copy(out=yfm[:, 0:4, :], in_=Bk[2][:].rearrange("p (a n) -> p a n", a=4))
            A.wait_ge(s_pe, ms[f"pe_y2_{seg}_{gg}"])
            ai = A.copy(out=yfm[:, 4:8, :], in_=Bk[3][:].rearrange("p (a n) -> p a n", a=4))
            ac_mark(ai); ms[f"ac_ye_{seg}_{gg}"] = cnt["ac"]
            PE.wait_ge(s_dv, ms[f"dv_kln_{seg}_{gg}"])
            if gg >= 1:
                PE.wait_ge(s_dv, ms[f"dv_Ssb_{seg}_{gg-1}"])
            for h in range(16):
                par = h % 2
                mi = PE.matmul(Bk[7][par * 64:par * 64 + 64,
                                    (h // 2) * 64:(h // 2) * 64 + 64],
                               lhsT=kln_tm[:, h * 64:(h + 1) * 64],
                               rhs=v_tm[:, h * 64:(h + 1) * 64],
                               start=(gg == 0 and h < 2), stop=True,
                               tile_position=(0, par * 64), skip_group_check=True)
            pe_mark(mi); ms[f"pe_upd_{seg}_{gg}"] = cnt["pe"]
            V.wait_ge(s_pe, ms[f"pe_upd_{seg}_{gg}"])
            vo("Ssb", ["Scrm", "Scr"], lambda: V.tensor_tensor(
                out=S_sb[:], in0=Bk[7][:], in1=S_cr[:], op=ALU.add))
            dv_inc(); ms[f"dv_Ssb_{seg}_{gg}"] = cnt["dv"]

        def a_wo(gg):
            PE.wait_ge(s_ac, ms[f"ac_ye_{seg}_{gg}"])
            PE.wait_ge(s_dv, ms[f"dv_mask2_{seg}_{gg}"])
            for kk in range(8):
                for n2 in range(2):
                    mi = PE.matmul(Bk[4 + n2][:, :], lhsT=yfm[:, kk, :],
                                   rhs=wo[:, kk, n2 * 512:n2 * 512 + 512],
                                   start=(kk == 0), stop=(kk == 7))
            pe_mark(mi); ms[f"pe_wo_{seg}_{gg}"] = cnt["pe"]

        def a_ln2(gg):
            r = gg % 4
            xb = x_tm[:, gg % 2, :]
            V.wait_ge(s_pe, ms[f"pe_wo_{seg}_{gg}"])
            vo("bnsA", [], lambda: V.bn_stats(out=bns[:, 0:6], in_=Bk[4][:]))
            vo("bnsB", [], lambda: V.bn_stats(out=bns[:, 6:12], in_=Bk[5][:]))
            vo("mv2", [], lambda: V.bn_aggr(out=mv12[:, r, 2:4], in_=bns[:]))
            dv_inc(); ms[f"dv_ln2s_{seg}_{gg}"] = cnt["dv"]
            A.wait_ge(s_dv, ms[f"dv_ln2s_{seg}_{gg}"])
            ai = A.activation(out=scal[:, r, SL_SD2:SL_SD2 + 1], in_=mv12[:, r, 3:4],
                              func=AF.Sqrt, bias=epsv[:])
            ac_mark(ai); ms[f"ac_sd2_{seg}_{gg}"] = cnt["ac"]
            V.wait_ge(s_ac, ms[f"ac_sd2_{seg}_{gg}"])
            vo("r2", [], lambda: V.reciprocal(
                out=scal[:, r, SL_R2:SL_R2 + 1], in_=scal[:, r, SL_SD2:SL_SD2 + 1]))
            vo("b2s", [], lambda: V.tensor_scalar(
                out=scal[:, r, SL_B2:SL_B2 + 1], in0=mv12[:, r, 2:3],
                scalar1=scal[:, r, SL_R2:SL_R2 + 1], scalar2=-1.0,
                op0=ALU.mult, op1=ALU.mult))
            dv_inc(); ms[f"dv_ln2r_{seg}_{gg}"] = cnt["dv"]
            A.wait_ge(s_dv, ms[f"dv_ln2r_{seg}_{gg}"])
            A.activation(out=tmpa[:, 0:512], in_=Bk[4][:], func=AF.Identity,
                         scale=scal[:, r, SL_R2:SL_R2 + 1],
                         bias=scal[:, r, SL_B2:SL_B2 + 1])
            ai = A.activation(out=tmpa[:, 512:1024], in_=Bk[5][:], func=AF.Identity,
                              scale=scal[:, r, SL_R2:SL_R2 + 1],
                              bias=scal[:, r, SL_B2:SL_B2 + 1])
            ac_mark(ai); ms[f"ac_ln2a_{seg}_{gg}"] = cnt["ac"]

        def a_x2w(gg):
            xb = x_tm[:, gg % 2, :]
            V.wait_ge(s_ac, ms[f"ac_ln2a_{seg}_{gg}"])
            if seg > 0:
                wait_gd(V, ms[f"gd_out_{seg-1}_{gg}"])
            vo("x2w", [], lambda: V.tensor_tensor(
                out=x2[:, gg, :], in0=tmpa[:], in1=xb, op=ALU.add))
            dv_inc(); ms[f"dv_x2_{seg}_{gg}"] = cnt["dv"]

        a_load(0)
        if G > 1:
            a_load(1)
        a_front(0)
        a_front_tp(0)
        a_qk(0)
        a_v(0)
        a_kchain(0)
        a_kapp_half(0, 1)
        a_vevac(0)
        for gg in range(G):
            a_sc(gg)
            if gg + 1 < G:
                a_front(gg + 1)
                a_front_tp(gg + 1)
            a_y(gg)
            if gg + 1 < G:
                a_qk(gg + 1)
            a_wo(gg)
            if gg + 1 < G:
                a_v(gg + 1)
                a_kchain(gg + 1)
            a_ln2(gg)
            if gg + 1 < G:
                a_kapp_half(gg + 1, 1)
                a_vevac(gg + 1)
            a_x2w(gg)
            if gg + 2 < G:
                a_load(gg + 2)

        V.wait_ge(s_pe, ms[f"pe_upd_{seg}_{G-1}"])
        vo("Scr", [], lambda: V.tensor_tensor(
            out=S_cr[:], in0=Bk[7][:], in1=S_cr[:], op=ALU.add))
        dv_inc(); ms[f"dv_Scr_{seg}"] = cnt["dv"]
        ms[f"pe_attn_done_{seg}"] = cnt["pe"]

        GP.wait_ge(s_pe, ms[f"pe_attn_done_{seg}"])
        GP.wait_ge(s_dv, ms[f"dv_x2_{seg}_{G-1}"])
        if seg == 0:
            for i_ in range(8):
                GP.wait_ge(s_g[i_], gslot[i_])
        w1q, w2q = open_weights("ffn")
        ms[f"ffnw_{seg}"] = [gd_dma(w1q, w1q_h), gd_dma(w2q, w2q_h)]

        def f_front(gg):
            r = gg % 4
            sl = gg % 2
            h3 = h3r[:, sl, :]
            h4 = h4r[:, sl, :]
            fw = fwk[:, sl, :]
            if gg == 0:
                V.wait_ge(s_dv, ms[f"dv_Scr_{seg}"])
            vo("bnsA", [], lambda: V.bn_stats(out=bns[:, 0:6], in_=x2[:, gg, 0:512]))
            vo("bnsB", [], lambda: V.bn_stats(out=bns[:, 6:12], in_=x2[:, gg, 512:1024]))
            vo("mv3", [], lambda: V.bn_aggr(out=mvf[:, r, 0:2], in_=bns[:]))
            dv_inc(); ms[f"dv_ln3s_{seg}_{gg}"] = cnt["dv"]
            A.wait_ge(s_dv, ms[f"dv_ln3s_{seg}_{gg}"])
            ai = A.activation(out=scal[:, r, SL_T0:SL_T0 + 1], in_=mvf[:, r, 1:2],
                              func=AF.Sqrt, bias=epsv[:])
            ac_mark(ai); ms[f"ac_sd3_{seg}_{gg}"] = cnt["ac"]
            V.wait_ge(s_ac, ms[f"ac_sd3_{seg}_{gg}"])
            vo("r3", [], lambda: V.reciprocal(
                out=scal[:, r, SL_T1:SL_T1 + 1], in_=scal[:, r, SL_T0:SL_T0 + 1]))
            vo("b3s", [], lambda: V.tensor_scalar(
                out=scal[:, r, SL_T2:SL_T2 + 1], in0=mvf[:, r, 0:1],
                scalar1=scal[:, r, SL_T1:SL_T1 + 1], scalar2=-1.0,
                op0=ALU.mult, op1=ALU.mult))
            dv_inc(); ms[f"dv_ln3r_{seg}_{gg}"] = cnt["dv"]
            A.wait_ge(s_dv, ms[f"dv_ln3r_{seg}_{gg}"])
            ai = A.activation(out=h3, in_=x2[:, gg, :], func=AF.Identity,
                              scale=scal[:, r, SL_T1:SL_T1 + 1],
                              bias=scal[:, r, SL_T2:SL_T2 + 1])
            ac_mark(ai); ms[f"ac_h3_{seg}_{gg}"] = cnt["ac"]
            V.wait_ge(s_ac, ms[f"ac_h3_{seg}_{gg}"])
            vo("bnsA", [], lambda: V.bn_stats(out=bns[:, 0:6], in_=h3[:, 0:512]))
            vo("bnsB", [], lambda: V.bn_stats(out=bns[:, 6:12], in_=h3[:, 512:1024]))
            vo("mv4", [], lambda: V.bn_aggr(out=mvf[:, r, 2:4], in_=bns[:]))
            dv_inc(); ms[f"dv_ln4s_{seg}_{gg}"] = cnt["dv"]
            A.wait_ge(s_dv, ms[f"dv_ln4s_{seg}_{gg}"])
            ai = A.activation(out=scal[:, r, SL_T3:SL_T3 + 1], in_=mvf[:, r, 3:4],
                              func=AF.Sqrt, bias=epsv[:])
            ac_mark(ai); ms[f"ac_sd4_{seg}_{gg}"] = cnt["ac"]
            V.wait_ge(s_ac, ms[f"ac_sd4_{seg}_{gg}"])
            vo("r4", [], lambda: V.reciprocal(
                out=scal[:, r, SL_QM1:SL_QM1 + 1], in_=scal[:, r, SL_T3:SL_T3 + 1]))
            vo("tg", ["r4"], lambda: V.tensor_scalar(
                out=fw, in0=h3, scalar1=mvf[:, r, 2:3],
                scalar2=scal[:, r, SL_QM1:SL_QM1 + 1],
                op0=ALU.subtract, op1=ALU.mult))
            vo("h4a", ["tg"], lambda: V.tensor_tensor(
                out=h4, in0=fw, in1=gb[:], op=ALU.mult))
            vo("h4b", ["h4a"], lambda: V.tensor_tensor(
                out=h4, in0=h4, in1=bbt[:], op=ALU.add))
            vo("qm1", ["h4b"], lambda: V.tensor_reduce(
                out=scal[:, r, SL_QM1:SL_QM1 + 1], in_=h4, axis=AX.X, op=ALU.max,
                apply_absolute_value=True))
            vo("qm1c", ["qm1"], lambda: V.tensor_scalar(
                out=scal[:, r, SL_T0:SL_T0 + 1], in0=scal[:, r, SL_QM1:SL_QM1 + 1],
                scalar1=EPS, scalar2=None, op0=ALU.max))
            vo("dq1", ["qm1c"], lambda: V.tensor_scalar(
                out=scal[:, r, SL_DQ1:SL_DQ1 + 1], in0=scal[:, r, SL_T0:SL_T0 + 1],
                scalar1=swc[:, 1:2], scalar2=None, op0=ALU.mult))
            vo("iq1", ["qm1c"], lambda: V.reciprocal(
                out=scal[:, r, SL_T1:SL_T1 + 1], in_=scal[:, r, SL_T0:SL_T0 + 1]))
            vo("s1", ["iq1"], lambda: V.tensor_scalar(
                out=scal[:, r, SL_S1:SL_S1 + 1], in0=scal[:, r, SL_T1:SL_T1 + 1],
                scalar1=127.0, scalar2=None, op0=ALU.mult))
            vo("ss1", ["s1"], lambda: V.tensor_scalar(
                out=scal[:, r, SL_SS1:SL_SS1 + 1], in0=scal[:, r, SL_S1:SL_S1 + 1],
                scalar1=swc[:, 0:1], scalar2=None, op0=ALU.mult))
            vo("y1", ["s1"], lambda: V.tensor_scalar(
                out=fw, in0=h4, scalar1=scal[:, r, SL_S1:SL_S1 + 1],
                scalar2=mw[:], op0=ALU.mult, op1=ALU.add))
            if gg >= 2:
                V.wait_ge(s_tp, ms[f"tp_xq1_{seg}_{gg-2}"])
            vo("xq1", ["y1"], lambda: V.tensor_scalar(
                out=xq1_tm[:, sl, :], in0=fw, scalar1=mw[:], scalar2=None,
                op0=ALU.subtract))
            dv_inc(); ms[f"dv_xq1_{seg}_{gg}"] = cnt["dv"]

        def f_front_tp(gg):
            r = gg % 4
            sl = gg % 2
            PE.wait_ge(s_dv, ms[f"dv_xq1_{seg}_{gg}"])
            if gg >= 2:
                PE.wait_ge(s_ac, ms[f"ac_s1row_{seg}_{gg-2}"])
            mi = PE.transpose(Bk[6][0:1, sl * 128:sl * 128 + 128],
                              in_=scal[:, r, SL_SS1:SL_SS1 + 1], identity=identf[:])
            pe_mark(mi); ms[f"pe_s1t_{seg}_{gg}"] = cnt["pe"]
            A.wait_ge(s_pe, ms[f"pe_s1t_{seg}_{gg}"])
            ai = A.copy(out=s1rowr[:, sl, :], in_=Bk[6][0:1, sl * 128:sl * 128 + 128])
            ac_mark(ai); ms[f"ac_s1row_{seg}_{gg}"] = cnt["ac"]
            SY.wait_ge(s_dv, ms[f"dv_xq1_{seg}_{gg}"])
            if gg >= 2:
                SY.wait_ge(s_pe, ms[f"pe_f1_{seg}_{gg-2}_3"])
            tp_dma(xq1_fm[:, sl, :, :], xq1_tm[:, sl, :])
            ms[f"tp_xq1_{seg}_{gg}"] = cnt["tp"]

        def f_f1(gg):
            r = gg % 4
            sl = gg % 2
            xf = xq1_fm[:, sl, :, :]
            if gg == 0:
                for h_ in ms[f"ffnw_{seg}"]:
                    wait_gd(PE, h_)
            PE.wait_ge(s_tp, ms[f"tp_xq1_{seg}_{gg}"])
            PE.wait_ge(s_ac, ms[f"ac_s1row_{seg}_{gg}"])
            for slab in range(4):
                bb0 = 0 if slab % 2 == 0 else 2
                key = (f"ac_gel_{seg}_{gg}_{slab-2}" if slab >= 2 else
                       (f"ac_gel_{seg}_{gg-1}_{slab+2}" if gg >= 1 else None))
                if key is not None:
                    PE.wait_ge(s_ac, ms[key])
                for n2 in range(2):
                    hs0 = slab * 1024 + n2 * 512
                    for kk in range(8):
                        PE.matmul(Bk[bb0 + n2][:, :], lhsT=xf[:, kk, :],
                                  rhs=w1q[:, kk, hs0:hs0 + 512],
                                  start=(kk == 0), stop=False)
                    mi = PE.matmul(Bk[bb0 + n2][:, :], lhsT=s1rowr[:, sl, :],
                                   rhs=b1r[:, hs0:hs0 + 512], start=False, stop=True)
                pe_mark(mi); ms[f"pe_f1_{seg}_{gg}_{slab}"] = cnt["pe"]
                A.wait_ge(s_pe, ms[f"pe_f1_{seg}_{gg}_{slab}"])
                if gg >= 1:
                    A.wait_ge(s_dv, ms[f"dv_xq2c_{seg}_{gg-1}_{slab}"])
                for n2 in range(2):
                    hs0 = slab * 1024 + n2 * 512
                    ai = A.activation(out=g_sb[:, hs0:hs0 + 512], in_=Bk[bb0 + n2][:],
                                      func=AF.Gelu, scale=scal[:, r, SL_DQ1:SL_DQ1 + 1])
                ac_mark(ai); ms[f"ac_gel_{seg}_{gg}_{slab}"] = cnt["ac"]
                V.wait_ge(s_ac, ms[f"ac_gel_{seg}_{gg}_{slab}"])
                vo(f"qp{slab}", [], lambda slab=slab: V.tensor_reduce(
                    out=khs[:, r, 60 + slab:61 + slab],
                    in_=g_sb[:, slab * 1024:(slab + 1) * 1024],
                    axis=AX.X, op=ALU.max, apply_absolute_value=True))

        def f_q2(gg):
            r = gg % 4
            vo("qm2", ["qp0", "qp1", "qp2", "qp3"], lambda: V.tensor_reduce(
                out=scal[:, r, SL_QM2:SL_QM2 + 1], in_=khs[:, r, 60:64],
                axis=AX.X, op=ALU.max))
            vo("qm2c", ["qm2"], lambda: V.tensor_scalar(
                out=scal[:, r, SL_T2:SL_T2 + 1], in0=scal[:, r, SL_QM2:SL_QM2 + 1],
                scalar1=EPS, scalar2=None, op0=ALU.max))
            vo("dq2", ["qm2c"], lambda: V.tensor_scalar(
                out=scal[:, r, SL_DQ2:SL_DQ2 + 1], in0=scal[:, r, SL_T2:SL_T2 + 1],
                scalar1=swc[:, 3:4], scalar2=None, op0=ALU.mult))
            vo("iq2", ["qm2c"], lambda: V.reciprocal(
                out=scal[:, r, SL_T3:SL_T3 + 1], in_=scal[:, r, SL_T2:SL_T2 + 1]))
            vo("s2", ["iq2"], lambda: V.tensor_scalar(
                out=scal[:, r, SL_S2:SL_S2 + 1], in0=scal[:, r, SL_T3:SL_T3 + 1],
                scalar1=127.0, scalar2=None, op0=ALU.mult))
            vo("ss2", ["s2"], lambda: V.tensor_scalar(
                out=scal[:, r, SL_SS2:SL_SS2 + 1], in0=scal[:, r, SL_S2:SL_S2 + 1],
                scalar1=swc[:, 2:3], scalar2=None, op0=ALU.mult))
            dv_inc(); ms[f"dv_s2_{seg}_{gg}"] = cnt["dv"]
            for hh in range(4):
                if gg >= 1:
                    V.wait_ge(s_tp, ms[f"tp_xq2_{seg}_{gg-1}_{hh // 2}"])
                vo(f"y2{hh}", ["s2"], lambda hh=hh: V.tensor_scalar(
                    out=y2, in0=g_sb[:, hh * 1024:(hh + 1) * 1024],
                    scalar1=scal[:, r, SL_S2:SL_S2 + 1], scalar2=mw[:],
                    op0=ALU.mult, op1=ALU.add))
                vo(f"xq2{hh}", [f"y2{hh}"], lambda hh=hh: V.tensor_scalar(
                    out=xq2_tm[:, hh * 1024:(hh + 1) * 1024], in0=y2,
                    scalar1=mw[:], scalar2=None, op0=ALU.subtract))
                dv_inc(); ms[f"dv_xq2c_{seg}_{gg}_{hh}"] = cnt["dv"]
                if hh % 2 == 1:
                    half = hh // 2
                    ms[f"dv_xq2b_{seg}_{gg}_{half}"] = cnt["dv"]
                    SY.wait_ge(s_dv, cnt["dv"])
                    if gg >= 1:
                        SY.wait_ge(s_pe, ms[f"pe_f2_{seg}_{gg-1}_{half}"])
                    tp_dma(xq2_fm[:, half * 16:half * 16 + 16, :],
                           xq2_tm[:, half * 2048:(half + 1) * 2048])
                    ms[f"tp_xq2_{seg}_{gg}_{half}"] = cnt["tp"]

        def f_f2(gg):
            r = gg % 4
            sl = gg % 2
            PE.wait_ge(s_dv, ms[f"dv_s2_{seg}_{gg}"])
            if gg >= 2:
                PE.wait_ge(s_ac, ms[f"ac_s2row_{seg}_{gg-2}"])
            mi = PE.transpose(Bk[6][0:1, 256 + sl * 128:256 + sl * 128 + 128],
                              in_=scal[:, r, SL_SS2:SL_SS2 + 1], identity=identf[:])
            pe_mark(mi); ms[f"pe_s2t_{seg}_{gg}"] = cnt["pe"]
            A.wait_ge(s_pe, ms[f"pe_s2t_{seg}_{gg}"])
            ai = A.copy(out=s2rowr[:, sl, :],
                        in_=Bk[6][0:1, 256 + sl * 128:256 + sl * 128 + 128])
            ac_mark(ai); ms[f"ac_s2row_{seg}_{gg}"] = cnt["ac"]
            for half in range(2):
                PE.wait_ge(s_tp, ms[f"tp_xq2_{seg}_{gg}_{half}"])
                if half == 0:
                    PE.wait_ge(s_ac, ms[f"ac_s2row_{seg}_{gg}"])
                    if gg >= 1:
                        PE.wait_ge(s_ac, ms[f"ac_z_{seg}_{gg-1}"])
                for n2 in range(2):
                    for kk in range(half * 16, half * 16 + 16):
                        mi = PE.matmul(Bk[4 + n2][:, :], lhsT=xq2_fm[:, kk, :],
                                       rhs=w2q[:, kk, n2 * 512:n2 * 512 + 512],
                                       start=(kk == 0), stop=False)
                    if half == 1:
                        mi = PE.matmul(Bk[4 + n2][:, :], lhsT=s2rowr[:, sl, :],
                                       rhs=b2r[:, n2 * 512:n2 * 512 + 512],
                                       start=False, stop=True)
                pe_mark(mi); ms[f"pe_f2_{seg}_{gg}_{half}"] = cnt["pe"]
            ms[f"pe_f2_{seg}_{gg}"] = ms[f"pe_f2_{seg}_{gg}_1"]
            zt = ztr[:, sl, :]
            A.wait_ge(s_pe, ms[f"pe_f2_{seg}_{gg}"])
            if gg >= 2:
                A.wait_ge(s_dv, ms[f"dv_out_{seg}_{gg-2}"])
            A.activation(out=zt[:, 0:512], in_=Bk[4][:], func=AF.Identity,
                         scale=scal[:, r, SL_DQ2:SL_DQ2 + 1])
            ai = A.activation(out=zt[:, 512:1024], in_=Bk[5][:], func=AF.Identity,
                              scale=scal[:, r, SL_DQ2:SL_DQ2 + 1])
            ac_mark(ai); ms[f"ac_z_{seg}_{gg}"] = cnt["ac"]

        def f_tail(gg):
            r = gg % 4
            zt = ztr[:, gg % 2, :]
            znt = zt
            V.wait_ge(s_ac, ms[f"ac_z_{seg}_{gg}"])
            vo("bnsA", [], lambda: V.bn_stats(out=bns[:, 0:6], in_=zt[:, 0:512]))
            vo("bnsB", [], lambda: V.bn_stats(out=bns[:, 6:12], in_=zt[:, 512:1024]))
            vo("mv5", [], lambda: V.bn_aggr(out=mv5[:, r, 0:2], in_=bns[:]))
            dv_inc(); ms[f"dv_ln5s_{seg}_{gg}"] = cnt["dv"]
            A.wait_ge(s_dv, ms[f"dv_ln5s_{seg}_{gg}"])
            ai = A.activation(out=scal[:, r, SL_SD5:SL_SD5 + 1], in_=mv5[:, r, 1:2],
                              func=AF.Sqrt, bias=epsv[:])
            ac_mark(ai); ms[f"ac_sd5_{seg}_{gg}"] = cnt["ac"]
            V.wait_ge(s_ac, ms[f"ac_sd5_{seg}_{gg}"])
            vo("r5", [], lambda: V.reciprocal(
                out=scal[:, r, SL_R5:SL_R5 + 1], in_=scal[:, r, SL_SD5:SL_SD5 + 1]))
            vo("b5s", [], lambda: V.tensor_scalar(
                out=scal[:, r, SL_B5:SL_B5 + 1], in0=mv5[:, r, 0:1],
                scalar1=scal[:, r, SL_R5:SL_R5 + 1], scalar2=-1.0,
                op0=ALU.mult, op1=ALU.mult))
            dv_inc(); ms[f"dv_ln5r_{seg}_{gg}"] = cnt["dv"]
            A.wait_ge(s_dv, ms[f"dv_ln5r_{seg}_{gg}"])
            ai = A.activation(out=znt, in_=zt, func=AF.Identity,
                              scale=scal[:, r, SL_R5:SL_R5 + 1],
                              bias=scal[:, r, SL_B5:SL_B5 + 1])
            ac_mark(ai); ms[f"ac_zn_{seg}_{gg}"] = cnt["ac"]
            V.wait_ge(s_ac, ms[f"ac_zn_{seg}_{gg}"])
            vo("outw", [], lambda: V.tensor_tensor(
                out=x2[:, gg, :], in0=znt, in1=x2[:, gg, :], op=ALU.add))
            dv_inc(); ms[f"dv_out_{seg}_{gg}"] = cnt["dv"]
            GP.wait_ge(s_dv, ms[f"dv_out_{seg}_{gg}"])
            ms[f"gd_out_{seg}_{gg}"] = gd_dma(
                out_h[(g0 + gg) * 128:(g0 + gg) * 128 + 128, :], x2[:, gg, :])

        f_front(0)
        f_front_tp(0)
        if G > 1:
            f_front(1)
        f_f1(0)
        for gg in range(G):
            if gg >= 1:
                f_tail(gg - 1)
            if gg + 1 < G:
                f_front_tp(gg + 1)
            f_q2(gg)
            if gg + 2 < G:
                f_front(gg + 2)
            if gg + 1 < G:
                f_f1(gg + 1)
            f_f2(gg)
        f_tail(G - 1)
        ms[f"pe_ffn_done_{seg}"] = cnt["pe"]

    for i in range(8):
        SY.wait_ge(s_g[i], gslot[i])
    for i_ in (1, 0):
        if _wcm[i_] is not None:
            _wcm[i_].__exit__(None, None, None)
            _wcm[i_] = None
    es.close()
    return nc


def kernel(L0_attn_x, L0_attn_W_qkv, L0_attn_W_o, L0_ffn_w1, L0_ffn_b1,
           L0_ffn_w2, L0_ffn_b2, L0_ffn_gamma, L0_ffn_beta):
    x = np.ascontiguousarray(np.asarray(L0_attn_x, dtype=np.float32))
    B, L, Dm = x.shape
    nc = build(NT=L)
    common = {
        "w_qkv": np.ascontiguousarray(L0_attn_W_qkv, dtype=np.float32),
        "w_o": np.ascontiguousarray(L0_attn_W_o, dtype=np.float32),
        "w1": np.ascontiguousarray(L0_ffn_w1, dtype=np.float32),
        "b1": np.ascontiguousarray(L0_ffn_b1, dtype=np.float32).reshape(1, -1),
        "w2": np.ascontiguousarray(L0_ffn_w2, dtype=np.float32),
        "b2": np.ascontiguousarray(L0_ffn_b2, dtype=np.float32).reshape(1, -1),
        "gamma": np.ascontiguousarray(L0_ffn_gamma, dtype=np.float32).reshape(1, -1),
        "beta": np.ascontiguousarray(L0_ffn_beta, dtype=np.float32).reshape(1, -1),
    }
    in_maps = [{"x": np.ascontiguousarray(x[b]), **common} for b in range(B)]
    res = run_bass_kernel_spmd(nc, in_maps, core_ids=list(range(B)))
    return np.stack([res.results[b]["out"] for b in range(B)], axis=0)


# version 23
# speedup vs baseline: 1.0434x; 1.0434x over previous
import numpy as np
import contextlib
import concourse.bass as bass
import concourse.mybir as mybir
from concourse import library_config
from concourse.bass_isa import ReduceOp
from concourse.bass_utils import run_bass_kernel_spmd

F32 = mybir.dt.float32
BF16 = mybir.dt.bfloat16
FP8 = mybir.dt.float8e4
I32 = mybir.dt.int32
AF = mybir.ActivationFunctionType
ALU = mybir.AluOpType
AX = mybir.AxisListType

M32 = 12582912.0
C52 = 6755399441055744.0
EPS = 1e-5
D = 1024
HID = 4096
KB = 1024


TP_EXCLUSIVE = True


def build(NT=4096, NSEG=4):
    GT = NT // 128
    G = GT // NSEG
    assert G * NSEG * 128 == NT

    nc = bass.Bass("TRN2", target_bir_lowering=False)

    x_h = nc.dram_tensor("x", [NT, D], F32, kind="ExternalInput").ap()
    wqkv_h = nc.dram_tensor("w_qkv", [D, 3 * D], F32, kind="ExternalInput").ap()
    wo_h = nc.dram_tensor("w_o", [D, D], F32, kind="ExternalInput").ap()
    w1_h = nc.dram_tensor("w1", [HID, D], F32, kind="ExternalInput").ap()
    b1_h = nc.dram_tensor("b1", [1, HID], F32, kind="ExternalInput").ap()
    w2_h = nc.dram_tensor("w2", [D, HID], F32, kind="ExternalInput").ap()
    b2_h = nc.dram_tensor("b2", [1, D], F32, kind="ExternalInput").ap()
    gamma_h = nc.dram_tensor("gamma", [1, D], F32, kind="ExternalInput").ap()
    beta_h = nc.dram_tensor("beta", [1, D], F32, kind="ExternalInput").ap()
    out_h = nc.dram_tensor("out", [NT, D], F32, kind="ExternalOutput").ap()
    qkvs_h = nc.dram_tensor("qkvs", [D, 3 * D], BF16, kind="Internal").ap()
    wos_h = nc.dram_tensor("wos", [D, D], BF16, kind="Internal").ap()
    w1q_h = nc.dram_tensor("w1qs", [128, 8, HID], FP8, kind="Internal").ap()
    w2q_h = nc.dram_tensor("w2qs", [128, 32, D], FP8, kind="Internal").ap()

    es = contextlib.ExitStack()
    SB = lambda n, sh, dt=F32: es.enter_context(nc.sbuf_tensor(n, sh, dt))
    PS = lambda n, sh: es.enter_context(nc.psum_tensor(n, sh, F32))

    x2 = SB("x2", [128, G, D])
    gb = SB("gb", [128, D])
    bbt = SB("bbt", [128, D])
    mask4 = SB("mask4", [128, 4, 128], BF16)
    identf = SB("identf", [128, 128], F32)
    identb = SB("identb", [128, 128], BF16)
    b1r = SB("b1r", [1, HID], BF16)
    b2r = SB("b2r", [1, D], BF16)
    s1rowr = SB("s1rowr", [1, 2, 128], BF16)
    s2rowr = SB("s2rowr", [1, 2, 128], BF16)
    S_sb = SB("S_sb", [128, 8, 64], BF16)
    S_cr = SB("S_cr", [128, 512])
    mw = SB("mw", [128, 1])
    epsv = SB("epsv", [128, 1])
    swc = SB("swc", [128, 8])
    scal = SB("scal", [128, 4, 24])
    (SL_SD1, SL_R1, SL_B1, SL_SD2, SL_R2, SL_B2,
     SL_QM1, SL_S1, SL_DQ1, SL_SS1, SL_QM2, SL_S2, SL_DQ2, SL_SS2,
     SL_SD5, SL_R5, SL_B5, SL_T0, SL_T1, SL_T2, SL_T3) = range(21)
    mv12 = SB("mv12", [128, 4, 4])
    bns = SB("bns", [128, 12])
    khs = SB("khs", [128, 4, 64])
    mvf = SB("mvf", [128, 4, 4])
    mv5 = SB("mv5", [128, 4, 2])
    iid = SB("iid", [128, 128], I32)
    pw8t = SB("pw8t", [128, 4096], FP8)
    onesr = SB("onesr", [1, 128])
    onesc = SB("onesc", [128, 1])

    _wcm = [None, None]
    _wn = [0]

    def open_weights(kind):
        for i_ in (1, 0):
            if _wcm[i_] is not None:
                _wcm[i_].__exit__(None, None, None)
                _wcm[i_] = None
        _wn[0] += 1
        if kind == "attn":
            c0 = nc.sbuf_tensor(f"wqkv_t{_wn[0]}", [128, 8, 3072], BF16)
            c1 = nc.sbuf_tensor(f"wo_t{_wn[0]}", [128, 8, 1024], BF16)
        else:
            c0 = nc.sbuf_tensor(f"w1q_t{_wn[0]}", [128, 8, HID], FP8)
            c1 = nc.sbuf_tensor(f"w2q_t{_wn[0]}", [128, 32, D], FP8)
        _wcm[0], _wcm[1] = c0, c1
        return c0.__enter__().ap(), c1.__enter__().ap()

    af = SB("af32", [128, 13312])
    bf = SB("abf16", [128, 12288], BF16)

    def fv(e0, n, chunks=None):
        v = af[:, e0:e0 + n]
        return v if chunks is None else v.rearrange("p (c n) -> p c n", c=chunks)

    def bv(e0, n, chunks=None):
        v = bf[:, e0:e0 + n]
        return v if chunks is None else v.rearrange("p (c n) -> p c n", c=chunks)

    x_tm = fv(0, 2 * D, 2)
    k2sq = fv(2048, D)
    tmpa = fv(3072, D)
    h1_tm = bv(0, 2 * D, 2)
    h1_fm = bv(2048, 2048).rearrange("p (a c n) -> p a c n", a=2, c=8)
    qfm = bv(4096, 1024, 8)
    kln_tm = bv(5120, D)
    kln_fm = bv(6144, 2048).rearrange("p (a c n) -> p a c n", a=2, c=8)
    v_tm = bv(8192, D)
    scT = bv(9216, 2048).rearrange("p (e q n) -> p e q n", e=2, q=8)
    yfm = bv(11264, 1024, 8)
    h3r = fv(0, 2 * D, 2)
    h4r = fv(2048, 2 * D, 2)
    fwk = fv(4096, 2 * D, 2)
    g_sb = fv(6144, HID)
    y2 = fv(10240, 1024)
    ztr = fv(11264, 2 * D, 2)
    xq1_tm = bv(0, 2 * D, 2)
    xq1_fm = bv(2048, 2048).rearrange("p (a c n) -> p a c n", a=2, c=8)
    xq2_tm = bv(4096, HID)
    xq2_fm = bv(8192, 4096, 32)
    pw_f = fv(0, 4096)
    pw_f2 = fv(4096, 4096)
    pacc = fv(8192, 40)
    pw_b = bv(0, 4096)
    pw_t1 = bv(4096, 2048, 16)
    pw_t2 = bv(4096, 4096, 32)
    pw_81 = pw8t[:, 0:2048].rearrange("p (c n) -> p c n", c=16)
    pw_82 = pw8t[:, 0:4096].rearrange("p (c n) -> p c n", c=32)
    pw_t1b = bv(8192, 2048, 16)
    pw_bh = bv(8192, 4096)

    Bk = [PS(f"pb{i}", [128, 512]) for i in range(8)]

    s_g = [es.enter_context(nc.semaphore(f"s_g{i}")) for i in range(8)]
    s_tp = es.enter_context(nc.semaphore("s_tp"))
    s_pe = es.enter_context(nc.semaphore("s_pe"))
    s_dv = es.enter_context(nc.semaphore("s_dv"))
    s_ac = es.enter_context(nc.semaphore("s_ac"))
    s_gp = es.enter_context(nc.semaphore("s_gp"))

    blk = es.enter_context(nc.Block())
    PE, V, A, SY, GP = nc.tensor, nc.vector, nc.scalar, nc.sync, nc.gpsimd

    cnt = {"tp": 0, "pe": 0, "dv": 0, "ac": 0, "gp": 0}
    gslot = [0] * 8
    gring = [0]
    ms = {}

    def gd_dma(out, in_):
        i = gring[0] % 8
        gring[0] += 1
        GP.wait_ge(s_g[i], gslot[i])
        if TP_EXCLUSIVE:
            GP.wait_ge(s_tp, cnt["tp"])
        gslot[i] += 16
        GP.dma_start(out=out, in_=in_).then_inc(s_g[i], 16)
        return (i, gslot[i])

    def wait_gd(eng, h):
        eng.wait_ge(s_g[h[0]], h[1])

    def tp_dma(out, in_):
        if TP_EXCLUSIVE:
            for i in range(8):
                SY.wait_ge(s_g[i], gslot[i])
        SY.wait_ge(s_tp, cnt["tp"])
        cnt["tp"] += 16
        SY.dma_start_transpose(out=out, in_=in_).then_inc(s_tp, 16)

    def pe_inc():
        cnt["pe"] += 1
        PE.drain().then_inc(s_pe, 1)

    def pe_mark(inst):
        cnt["pe"] += 1
        inst.then_inc(s_pe, 1)

    def dv_inc():
        cnt["dv"] += 1
        _pend.clear()
        V.drain().then_inc(s_dv, 1)

    def ac_inc():
        cnt["ac"] += 1
        A.drain().then_inc(s_ac, 1)

    def ac_mark(inst):
        cnt["ac"] += 1
        inst.then_inc(s_ac, 1)

    _pend = set()

    def vdrain():
        V.drain()
        _pend.clear()

    def vo(tag, reads, emit):
        if _pend:
            vdrain()
        emit()
        _pend.add(tag)

    def vmemset(tag, ap, val):
        V.memset(ap, val)
        _pend.add(tag)

    GP.iota(out=iid[:], pattern=[[1, 128]], base=0, channel_multiplier=-1)
    GP.drain().then_inc(s_gp, 1); cnt["gp"] += 1

    wqkv, wo = open_weights("attn")
    hs = [gd_dma(gb[0:1, :], gamma_h),
          gd_dma(bbt[0:1, :], beta_h),
          gd_dma(b1r[:], b1_h),
          gd_dma(b2r[:], b2_h),
          gd_dma(wqkv, wqkv_h.rearrange("(c p) n -> p c n", p=128)),
          gd_dma(wo, wo_h.rearrange("(c p) n -> p c n", p=128))]
    for h_ in hs[4:]:
        wait_gd(GP, h_)
    gd_dma(qkvs_h.rearrange("(c p) n -> p c n", p=128), wqkv)
    gd_dma(wos_h.rearrange("(c p) n -> p c n", p=128), wo)

    V.wait_ge(s_gp, 1)
    vmemset("onesr", onesr[:], 1.0)
    vmemset("onesc", onesc[:], 1.0)
    dv_inc()
    PE.wait_ge(s_dv, cnt["dv"])
    for h_ in hs[0:2]:
        wait_gd(PE, h_)
    for n2 in range(2):
        PE.matmul(Bk[n2][:, :], lhsT=onesr[:], rhs=gb[0:1, n2 * 512:(n2 + 1) * 512],
                  start=True, stop=True, skip_group_check=True)
        PE.matmul(Bk[2 + n2][:, :], lhsT=onesr[:],
                  rhs=bbt[0:1, n2 * 512:(n2 + 1) * 512],
                  start=True, stop=True, skip_group_check=True)
    pe_inc()
    A.wait_ge(s_pe, cnt["pe"])
    A.copy(out=gb[:, 0:512], in_=Bk[0][:])
    A.copy(out=gb[:, 512:1024], in_=Bk[1][:])
    A.copy(out=bbt[:, 0:512], in_=Bk[2][:])
    A.copy(out=bbt[:, 512:1024], in_=Bk[3][:])
    ac_inc()
    vmemset("pilm", swc[:, 4:5], 0.75)
    vo("identf", ["iid"], lambda: V.tensor_scalar(
        out=identf[:], in0=iid[:], scalar1=0, scalar2=None, op0=ALU.is_equal))
    vo("identb", ["identf"], lambda: V.tensor_copy(out=identb[:], in_=identf[:]))
    for j in range(4):
        vo("mask4", ["iid"], lambda j=j: V.tensor_scalar(
            out=mask4[:, j, :], in0=iid[:], scalar1=0, scalar2=None, op0=ALU.is_ge))
    vo("pil", ["pilm"], lambda: V.tensor_scalar(
        out=swc[:, 4:5], in0=swc[:, 4:5], scalar1=C52, scalar2=C52,
        op0=ALU.add, op1=ALU.subtract))
    V.drain()
    vo("mw", ["pil"], lambda: V.tensor_scalar(
        out=mw[:], in0=swc[:, 4:5], scalar1=C52, scalar2=M32,
        op0=ALU.mult, op1=ALU.add))
    vmemset("epsm", epsv[:], EPS)
    vmemset("Ssbm", S_sb[:], 0)
    vmemset("Scrm", S_cr[:], 0)
    dv_inc(); ms["dv_identb"] = cnt["dv"]

    def prep_w(which):
        wsrc_t = (w1_h.rearrange("(a p) n -> p a n", p=128) if which == 1
                  else w2_h.rearrange("(a p) n -> p a n", p=128))
        nt = 32 if which == 1 else 8
        cols = D if which == 1 else HID
        per = 4 if which == 1 else 1
        wdst = w1q_h if which == 1 else w2q_h
        sw_s = swc[:, 0:1] if which == 1 else swc[:, 2:3]
        inv_s = swc[:, 1:2] if which == 1 else swc[:, 3:4]
        pw8 = pw_81 if which == 1 else pw_82
        nld = nt // per
        hs1 = []
        for i in range(nld):
            buf = pw_f if i % 2 == 0 else pw_f2
            src_ = (wsrc_t[:, i * per:(i + 1) * per, :] if per > 1
                    else wsrc_t[:, i, :])
            dstv = (buf.rearrange("p (a n) -> p a n", a=per) if per > 1 else buf)
            if i >= 2:
                GP.wait_ge(s_dv, ms[f"prep_pa_{which}_{i-2}"])
            hs1.append(gd_dma(dstv, src_))
            wait_gd(V, hs1[-1])
            vo(f"pa{i}", [], lambda i=i, dv=dstv: V.tensor_reduce(
                out=pacc[:, i:i + 1], in_=dv,
                axis=(AX.XY if per > 1 else AX.X), op=ALU.add,
                apply_absolute_value=True))
            dv_inc(); ms[f"prep_pa_{which}_{i}"] = cnt["dv"]
        vo("pt", [], lambda: V.tensor_reduce(
            out=pacc[:, 34:35], in_=pacc[:, 0:nld], axis=AX.X, op=ALU.add))
        dv_inc()
        PE.wait_ge(s_dv, cnt["dv"])
        PE.matmul(Bk[4][0:1, 0:1], lhsT=pacc[:, 34:35], rhs=onesc[:],
                  start=True, stop=True, skip_group_check=True)
        pe_inc()
        A.wait_ge(s_pe, cnt["pe"])
        A.copy(out=pacc[0:1, 38:39], in_=Bk[4][0:1, 0:1])
        ac_inc()
        PE.wait_ge(s_ac, cnt["ac"])
        PE.matmul(Bk[4][:, 0:1], lhsT=onesr[:], rhs=pacc[0:1, 38:39],
                  start=True, stop=True, skip_group_check=True)
        pe_inc()
        A.wait_ge(s_pe, cnt["pe"])
        A.copy(out=pacc[:, 35:36], in_=Bk[4][:, 0:1])
        ac_inc()
        V.wait_ge(s_ac, cnt["ac"])
        vo("wm", [], lambda: V.tensor_scalar(
            out=pacc[:, 36:37], in0=pacc[:, 35:36], scalar1=1.0 / (nt * 128 * cols),
            scalar2=EPS, op0=ALU.mult, op1=ALU.max))
        vo("sw", ["wm"], lambda: V.reciprocal(out=sw_s, in_=pacc[:, 36:37]))
        vo("s127", ["sw"], lambda: V.tensor_scalar(
            out=pacc[:, 37:38], in0=sw_s, scalar1=127.0, scalar2=None, op0=ALU.mult))
        vo("inv", ["s127"], lambda: V.reciprocal(out=inv_s, in_=pacc[:, 37:38]))
        dv_inc(); ms[f"prep_mean_{which}"] = cnt["dv"]
        def quant3(tag, fb, bb):
            vo(f"{tag}a", [], lambda b=fb: V.tensor_scalar(
                out=b, in0=b, scalar1=sw_s, scalar2=mw[:], op0=ALU.mult, op1=ALU.add))
            vo(f"{tag}b", [f"{tag}a"], lambda b=fb: V.tensor_scalar(
                out=b, in0=b, scalar1=mw[:], scalar2=None, op0=ALU.subtract))
            vo(f"{tag}c", [f"{tag}b"], lambda b=fb, o=bb: V.tensor_scalar(
                out=o, in0=b, scalar1=-1.0, scalar2=1.0, op0=ALU.max, op1=ALU.min))
        sth = {}
        if which == 1:
            fsl = ([pw_f[:, j * 1024:(j + 1) * 1024] for j in range(4)]
                   + [pw_f2[:, j * 1024:(j + 1) * 1024] for j in range(4)])
            tsl = ([pw_t1[:, j * 8:(j + 1) * 8, :] for j in range(2)]
                   + [pw_t1b[:, j * 8:(j + 1) * 8, :] for j in range(2)])
            ldh = {}
            for t in range(6):
                if t == 0:
                    GP.wait_ge(s_dv, ms[f"prep_mean_{which}"])
                ldh[t] = gd_dma(fsl[t % 8], wsrc_t[:, t, :])
            for t in range(32):
                s4, s8 = t % 4, t % 8
                wait_gd(V, ldh[t])
                quant3(f"q{which}_{t}", fsl[s8], fsl[s8])
                dv_inc(); ms[f"prep_q_{which}_{t}"] = cnt["dv"]
                PE.wait_ge(s_dv, cnt["dv"])
                for grp in range(2):
                    bank = 5 + grp
                    if t >= 1:
                        PE.wait_ge(s_ac, ms[f"prep_ev_{which}_{t-1}_{grp}"])
                    for q in range(4):
                        c = grp * 4 + q
                        mi = PE.transpose(Bk[bank][:, q * 128:(q + 1) * 128],
                                          in_=fsl[s8][:, c * 128:(c + 1) * 128],
                                          identity=identf[:])
                    pe_mark(mi); ms[f"prep_pt_{which}_{t}_{grp}"] = cnt["pe"]
                    A.wait_ge(s_pe, cnt["pe"])
                    if grp == 0 and t >= 4:
                        wait_gd(A, sth[t - 4])
                    ai = A.copy(out=tsl[s4][:, grp * 4:(grp + 1) * 4, :],
                                in_=Bk[bank][:].rearrange("p (a n) -> p a n", a=4))
                    ac_mark(ai); ms[f"prep_ev_{which}_{t}_{grp}"] = cnt["ac"]
                ms[f"prep_pt_{which}_{t}"] = ms[f"prep_pt_{which}_{t}_1"]
                if t + 6 < 32:
                    if t >= 2:
                        GP.wait_ge(s_pe, ms[f"prep_pt_{which}_{t-2}"])
                    ldh[t + 6] = gd_dma(fsl[(t + 6) % 8], wsrc_t[:, t + 6, :])
                GP.wait_ge(s_ac, ms[f"prep_ev_{which}_{t}_1"])
                sth[t] = gd_dma(wdst[:, :, t * 128:(t + 1) * 128], tsl[s4])
        else:
            fsl = [pw_f[:, 0:2048], pw_f[:, 2048:4096],
                   pw_f2[:, 0:2048], pw_f2[:, 2048:4096]]
            tsl = [pw_t2[:, 0:16, :], pw_t2[:, 16:32, :]]
            ldh = {}
            for ht in range(3):
                if ht == 0:
                    GP.wait_ge(s_dv, ms[f"prep_mean_{which}"])
                ldh[ht] = gd_dma(fsl[ht % 4],
                                 wsrc_t[:, ht // 2, (ht % 2) * 2048:(ht % 2) * 2048 + 2048])
            for ht in range(16):
                i_, hh = ht // 2, ht % 2
                wait_gd(V, ldh[ht])
                quant3(f"q{which}_{ht}", fsl[ht % 4], fsl[ht % 4])
                dv_inc(); ms[f"prep_q_{which}_{ht}"] = cnt["dv"]
                PE.wait_ge(s_dv, cnt["dv"])
                for grp in range(4):
                    bank = 5 + (grp % 2)
                    pt, pg = (ht, grp - 2) if grp >= 2 else (ht - 1, grp + 2)
                    if pt >= 0:
                        PE.wait_ge(s_ac, ms[f"prep_ev_{which}_{pt}_{pg}"])
                    elif grp < 2:
                        PE.wait_ge(s_ac, ms[f"prep_ev_1_31_{grp}"])
                    for q in range(4):
                        c = grp * 4 + q
                        mi = PE.transpose(Bk[bank][:, q * 128:(q + 1) * 128],
                                          in_=fsl[ht % 4][:, c * 128:(c + 1) * 128],
                                          identity=identf[:])
                    pe_mark(mi); ms[f"prep_pt2_{ht}_{grp}"] = cnt["pe"]
                    A.wait_ge(s_pe, cnt["pe"])
                    if grp == 0 and ht >= 2:
                        wait_gd(A, sth[ht - 2])
                    ai = A.copy(out=tsl[ht % 2][:, grp * 4:(grp + 1) * 4, :],
                                in_=Bk[bank][:].rearrange("p (a n) -> p a n", a=4))
                    ac_mark(ai); ms[f"prep_ev_{which}_{ht}_{grp}"] = cnt["ac"]
                ms[f"prep_pt_{which}_{ht}"] = ms[f"prep_pt2_{ht}_3"]
                if ht + 3 < 16:
                    if ht >= 1:
                        GP.wait_ge(s_pe, ms[f"prep_pt_{which}_{ht-1}"])
                    nh = ht + 3
                    ldh[nh] = gd_dma(
                        fsl[nh % 4],
                        wsrc_t[:, nh // 2, (nh % 2) * 2048:(nh % 2) * 2048 + 2048])
                GP.wait_ge(s_ac, ms[f"prep_ev_{which}_{ht}_3"])
                sth[ht] = gd_dma(
                    wdst[:, hh * 16:(hh + 1) * 16, i_ * 128:(i_ + 1) * 128],
                    tsl[ht % 2])
    prep_w(1)
    prep_w(2)
    ms["prep_dv"] = cnt["dv"]

    for seg in range(NSEG):
        g0 = seg * G
        if seg > 0:
            GP.wait_ge(s_pe, ms[f"pe_ffn_done_{seg-1}"])
            wqkv, wo = open_weights("attn")
            ms[f"attnw_{seg}"] = [
                gd_dma(wqkv, qkvs_h.rearrange("(c p) n -> p c n", p=128)),
                gd_dma(wo, wos_h.rearrange("(c p) n -> p c n", p=128))]
        else:
            ms[f"attnw_{seg}"] = []

        def a_load(gg):
            xb = x_tm[:, gg % 2, :]
            if gg >= 2:
                GP.wait_ge(s_dv, ms[f"dv_x2_{seg}_{gg-2}"])
            elif seg > 0:
                GP.wait_ge(s_dv, ms[f"dv_out_{seg-1}_{G-1}"])
            ms[f"gd_x_{seg}_{gg}"] = gd_dma(xb, x_h[(g0 + gg) * 128:(g0 + gg) * 128 + 128, :])

        def a_front(gg):
            r = gg % 4
            xb = x_tm[:, gg % 2, :]
            wait_gd(V, ms[f"gd_x_{seg}_{gg}"])
            vo("bnsA", [], lambda: V.bn_stats(out=bns[:, 0:6], in_=xb[:, 0:512]))
            vo("bnsB", [], lambda: V.bn_stats(out=bns[:, 6:12], in_=xb[:, 512:1024]))
            vo("mv1", [], lambda: V.bn_aggr(out=mv12[:, r, 0:2], in_=bns[:]))
            dv_inc(); ms[f"dv_ln1s_{seg}_{gg}"] = cnt["dv"]
            A.wait_ge(s_dv, ms[f"dv_ln1s_{seg}_{gg}"])
            ai = A.activation(out=scal[:, r, SL_SD1:SL_SD1 + 1], in_=mv12[:, r, 1:2],
                              func=AF.Sqrt, bias=epsv[:])
            ac_mark(ai); ms[f"ac_sd1_{seg}_{gg}"] = cnt["ac"]
            V.wait_ge(s_ac, ms[f"ac_sd1_{seg}_{gg}"])
            vo("r1", [], lambda: V.reciprocal(
                out=scal[:, r, SL_R1:SL_R1 + 1], in_=scal[:, r, SL_SD1:SL_SD1 + 1]))
            vo("b1s", [], lambda: V.tensor_scalar(
                out=scal[:, r, SL_B1:SL_B1 + 1], in0=mv12[:, r, 0:1],
                scalar1=scal[:, r, SL_R1:SL_R1 + 1], scalar2=-1.0,
                op0=ALU.mult, op1=ALU.mult))
            dv_inc(); ms[f"dv_ln1r_{seg}_{gg}"] = cnt["dv"]
            A.wait_ge(s_dv, ms[f"dv_ln1r_{seg}_{gg}"])
            if gg >= 2:
                A.wait_ge(s_tp, ms[f"tp_h1_{seg}_{gg-2}"])
            elif seg > 0:
                A.wait_ge(s_tp, ms[f"tp_xq2_{seg-1}_{G-1}_1"])
            hb = h1_tm[:, gg % 2, :]
            ai = A.activation(out=hb, in_=xb, func=AF.Identity,
                              scale=scal[:, r, SL_R1:SL_R1 + 1],
                              bias=scal[:, r, SL_B1:SL_B1 + 1])
            ac_mark(ai); ms[f"ac_h1_{seg}_{gg}"] = cnt["ac"]

        def a_front_tp(gg):
            SY.wait_ge(s_ac, ms[f"ac_h1_{seg}_{gg}"])
            if gg >= 2:
                SY.wait_ge(s_pe, ms[f"pe_v1_{seg}_{gg-2}"])
            tp_dma(h1_fm[:, gg % 2, :, :], h1_tm[:, gg % 2, :])
            ms[f"tp_h1_{seg}_{gg}"] = cnt["tp"]

        def a_qk(gg):
            hf = h1_fm[:, gg % 2, :, :]
            if gg == 0:
                for h_ in ms[f"attnw_{seg}"]:
                    wait_gd(PE, h_)
            PE.wait_ge(s_tp, ms[f"tp_h1_{seg}_{gg}"])
            if gg >= 1:
                PE.wait_ge(s_ac, ms[f"ac_ve1_{seg}_{gg-1}"])
                PE.wait_ge(s_pe, ms[f"pe_y2_{seg}_{gg-1}"])
            for m in range(8):
                pq = Bk[m // 4][:, (m % 4) * 128:(m % 4) * 128 + 128]
                for kk in range(8):
                    mi = PE.matmul(pq, lhsT=wqkv[:, kk, m * 128:(m + 1) * 128],
                                   rhs=hf[:, kk, :], start=(kk == 0), stop=(kk == 7))
            pe_mark(mi); ms[f"pe_q_{seg}_{gg}"] = cnt["pe"]
            if gg >= 1:
                PE.wait_ge(s_ac, ms[f"ac_ye_{seg}_{gg-1}"])
            for kk in range(8):
                for n2 in range(2):
                    mi = PE.matmul(Bk[2 + n2][:, :], lhsT=hf[:, kk, :],
                                   rhs=wqkv[:, kk, 1024 + n2 * 512:1536 + n2 * 512],
                                   start=(kk == 0), stop=(kk == 7))
            pe_mark(mi); ms[f"pe_k_{seg}_{gg}"] = cnt["pe"]
            A.wait_ge(s_pe, ms[f"pe_q_{seg}_{gg}"])
            A.copy(out=qfm[:, 0:4, :], in_=Bk[0][:].rearrange("p (a n) -> p a n", a=4))
            ai = A.copy(out=qfm[:, 4:8, :], in_=Bk[1][:].rearrange("p (a n) -> p a n", a=4))
            ac_mark(ai); ms[f"ac_qe_{seg}_{gg}"] = cnt["ac"]
            A.wait_ge(s_pe, ms[f"pe_k_{seg}_{gg}"])
            A.activation(out=k2sq[:, 0:512], in_=Bk[2][:], func=AF.Square)
            ai = A.activation(out=k2sq[:, 512:1024], in_=Bk[3][:], func=AF.Square)
            ac_mark(ai); ms[f"ac_k2_{seg}_{gg}"] = cnt["ac"]

        def a_v(gg):
            hf = h1_fm[:, gg % 2, :, :]
            PE.wait_ge(s_ac, ms[f"ac_qe_{seg}_{gg}"])
            for vh in range(2):
                for kk in range(8):
                    mi = PE.matmul(Bk[vh][:, :], lhsT=hf[:, kk, :],
                                   rhs=wqkv[:, kk, 2048 + vh * 512:2560 + vh * 512],
                                   start=(kk == 0), stop=(kk == 7))
                pe_mark(mi); ms[f"pe_v{vh}_{seg}_{gg}"] = cnt["pe"]

        def a_kchain(gg):
            r = gg % 4
            V.wait_ge(s_pe, ms[f"pe_k_{seg}_{gg}"])
            V.wait_ge(s_ac, ms[f"ac_k2_{seg}_{gg}"])
            kp2 = Bk[2][:].rearrange("p (h e) -> p h e", e=64)
            kp3 = Bk[3][:].rearrange("p (h e) -> p h e", e=64)
            vo("ks1", [], lambda: V.tensor_reduce(
                out=khs[:, r, 0:8], in_=kp2, axis=AX.X, op=ALU.add))
            vo("ks2", [], lambda: V.tensor_reduce(
                out=khs[:, r, 8:16], in_=kp3, axis=AX.X, op=ALU.add))
            vo("ks3", [], lambda: V.tensor_reduce(
                out=khs[:, r, 16:32],
                in_=k2sq.rearrange("p (h e) -> p h e", e=64), axis=AX.X, op=ALU.add))
            vo("kmu", [], lambda: V.tensor_scalar(
                out=khs[:, r, 32:48], in0=khs[:, r, 0:16], scalar1=1.0 / 64,
                scalar2=None, op0=ALU.mult))
            vo("ke2", [], lambda: V.tensor_scalar(
                out=khs[:, r, 48:64], in0=khs[:, r, 16:32], scalar1=1.0 / 64,
                scalar2=None, op0=ALU.mult))
            vo("kmsq", [], lambda: V.tensor_tensor(
                out=khs[:, r, 0:16], in0=khs[:, r, 32:48], in1=khs[:, r, 32:48],
                op=ALU.mult))
            vo("kvar", [], lambda: V.tensor_tensor(
                out=khs[:, r, 16:32], in0=khs[:, r, 48:64], in1=khs[:, r, 0:16],
                op=ALU.subtract))
            dv_inc(); ms[f"dv_kvar_{seg}_{gg}"] = cnt["dv"]
            A.wait_ge(s_dv, ms[f"dv_kvar_{seg}_{gg}"])
            ai = A.activation(out=khs[:, r, 48:64], in_=khs[:, r, 16:32],
                              func=AF.Sqrt, bias=epsv[:])
            ac_mark(ai); ms[f"ac_ksd_{seg}_{gg}"] = cnt["ac"]
            V.wait_ge(s_ac, ms[f"ac_ksd_{seg}_{gg}"])
            vo("krstd", [], lambda: V.reciprocal(
                out=khs[:, r, 16:32], in_=khs[:, r, 48:64]))
            if gg >= 1:
                V.wait_ge(s_tp, ms[f"tp_k1_{seg}_{gg-1}"])
                V.wait_ge(s_pe, ms[f"pe_upd_{seg}_{gg-1}"])
            a_kapp_half(gg, 0)

        def a_kapp_half(gg, half_):
            r = gg % 4
            kp2 = Bk[2][:].rearrange("p (h e) -> p h e", e=64)
            kp3 = Bk[3][:].rearrange("p (h e) -> p h e", e=64)
            kf = kln_fm[:, gg % 2, :, :]
            bk_ = kp2 if half_ == 0 else kp3
            mu0 = 32 + half_ * 8
            r0 = 16 + half_ * 8
            vo("kcen", ["kmu"], lambda bk_=bk_, mu0=mu0, half_=half_:
               V.tensor_tensor(
                out=k2sq[:, half_ * 512:half_ * 512 + 512].rearrange(
                    "p (h e) -> p h e", e=64),
                in0=bk_,
                in1=khs[:, r, mu0:mu0 + 8].unsqueeze(-1).broadcast_to([128, 8, 64]),
                op=ALU.subtract))
            vo("kapp", ["kcen", "krstd"], lambda half_=half_, r0=r0:
               V.tensor_tensor(
                out=kln_tm[:, half_ * 512:half_ * 512 + 512].rearrange(
                    "p (h e) -> p h e", e=64),
                in0=k2sq[:, half_ * 512:half_ * 512 + 512].rearrange(
                    "p (h e) -> p h e", e=64),
                in1=khs[:, r, r0:r0 + 8].unsqueeze(-1).broadcast_to([128, 8, 64]),
                op=ALU.mult))
            dv_inc(); ms[f"dv_kln{half_}_{seg}_{gg}"] = cnt["dv"]
            SY.wait_ge(s_dv, cnt["dv"])
            if gg >= 2:
                SY.wait_ge(s_pe, ms[f"pe_sc2_{seg}_{gg-2}"])
            tp_dma(kf[:, half_ * 4:half_ * 4 + 4, :],
                   kln_tm[:, half_ * 512:half_ * 512 + 512])
            ms[f"tp_k{half_}_{seg}_{gg}"] = cnt["tp"]
            if half_ == 1:
                ms[f"dv_kln_{seg}_{gg}"] = ms[f"dv_kln1_{seg}_{gg}"]

        def a_vevac(gg):
            A.wait_ge(s_pe, ms[f"pe_v0_{seg}_{gg}"])
            if gg >= 1:
                A.wait_ge(s_pe, ms[f"pe_upd_{seg}_{gg-1}"])
            ai = A.copy(out=v_tm[:, 0:512], in_=Bk[0][:])
            ac_mark(ai); ms[f"ac_ve0_{seg}_{gg}"] = cnt["ac"]
            A.wait_ge(s_pe, ms[f"pe_v1_{seg}_{gg}"])
            ai = A.copy(out=v_tm[:, 512:1024], in_=Bk[1][:])
            ac_mark(ai); ms[f"ac_ve1_{seg}_{gg}"] = cnt["ac"]
            ms[f"ac_ve_{seg}_{gg}"] = ms[f"ac_ve1_{seg}_{gg}"]

        def a_sc(gg):
            kf = kln_fm[:, gg % 2, :, :]
            PE.wait_ge(s_ac, ms[f"ac_qe_{seg}_{gg}"])
            PE.wait_ge(s_tp, ms[f"tp_k0_{seg}_{gg}"])
            if gg >= 1:
                PE.wait_ge(s_ac, ms[f"ac_ln2a_{seg}_{gg-1}"])
            for h in range(8):
                par = h % 2
                mi = PE.matmul(Bk[4 + par][:, (h // 2) * 128:(h // 2) * 128 + 128],
                               lhsT=kf[par * 64:par * 64 + 64, h // 2, :],
                               rhs=qfm[par * 64:par * 64 + 64, h // 2, :],
                               start=True, stop=True, tile_position=(par * 64, 0),
                               skip_group_check=True)
            pe_mark(mi); ms[f"pe_sc1_{seg}_{gg}"] = cnt["pe"]
            V.wait_ge(s_pe, ms[f"pe_sc1_{seg}_{gg}"])
            if gg >= 1:
                V.wait_ge(s_pe, ms[f"pe_y2_{seg}_{gg-1}"])
            for par in range(2):
                vo("scm1", [], lambda par=par: V.tensor_tensor(
                    out=scT[:, par, 0:4, :],
                    in0=Bk[4 + par][:].rearrange("p (a n) -> p a n", a=4),
                    in1=mask4[:], op=ALU.mult))
            dv_inc(); ms[f"dv_mask1_{seg}_{gg}"] = cnt["dv"]
            PE.wait_ge(s_tp, ms[f"tp_k1_{seg}_{gg}"])
            PE.wait_ge(s_dv, ms[f"dv_mask1_{seg}_{gg}"])
            for h in range(8, 16):
                par = h % 2
                fr = ((h - 8) // 2) * 128
                mi = PE.matmul(Bk[4 + par][:, fr:fr + 128],
                               lhsT=kf[par * 64:par * 64 + 64, h // 2, :],
                               rhs=qfm[par * 64:par * 64 + 64, h // 2, :],
                               start=True, stop=True, tile_position=(par * 64, 0),
                               skip_group_check=True)
            pe_mark(mi); ms[f"pe_sc2_{seg}_{gg}"] = cnt["pe"]
            V.wait_ge(s_pe, ms[f"pe_sc2_{seg}_{gg}"])
            for par in range(2):
                vo("scm2", [], lambda par=par: V.tensor_tensor(
                    out=scT[:, par, 4:8, :],
                    in0=Bk[4 + par][:].rearrange("p (a n) -> p a n", a=4),
                    in1=mask4[:], op=ALU.mult))
            dv_inc(); ms[f"dv_mask2_{seg}_{gg}"] = cnt["dv"]

        def a_y(gg):
            PE.wait_ge(s_dv, ms[f"dv_mask1_{seg}_{gg}"])
            PE.wait_ge(s_dv, ms[f"dv_kln0_{seg}_{gg}"])
            PE.wait_ge(s_ac, ms[f"ac_ve_{seg}_{gg}"])
            if gg >= 1:
                PE.wait_ge(s_dv, ms[f"dv_Ssb_{seg}_{gg-1}"])
            for h in range(8):
                par = h % 2
                dst = Bk[2][par * 64:par * 64 + 64,
                            (h // 2) * 128:(h // 2) * 128 + 128]
                PE.matmul(dst, lhsT=S_sb[par * 64:par * 64 + 64, h // 2, :],
                          rhs=qfm[par * 64:par * 64 + 64, h // 2, :],
                          start=True, stop=False,
                          tile_position=(par * 64, par * 64), skip_group_check=True)
                mi = PE.matmul(dst, lhsT=v_tm[:, h * 64:(h + 1) * 64],
                               rhs=scT[:, par, h // 2, :],
                               start=False, stop=True, tile_position=(0, par * 64),
                               skip_group_check=True)
            pe_mark(mi); ms[f"pe_y1_{seg}_{gg}"] = cnt["pe"]
            PE.wait_ge(s_dv, ms[f"dv_mask2_{seg}_{gg}"])
            PE.wait_ge(s_dv, ms[f"dv_kln_{seg}_{gg}"])
            for h in range(8, 16):
                par = h % 2
                fr = ((h - 8) // 2) * 128
                dst = Bk[3][par * 64:par * 64 + 64, fr:fr + 128]
                PE.matmul(dst, lhsT=S_sb[par * 64:par * 64 + 64, h // 2, :],
                          rhs=qfm[par * 64:par * 64 + 64, h // 2, :],
                          start=True, stop=False,
                          tile_position=(par * 64, par * 64), skip_group_check=True)
                mi = PE.matmul(dst, lhsT=v_tm[:, h * 64:(h + 1) * 64],
                               rhs=scT[:, par, h // 2, :],
                               start=False, stop=True, tile_position=(0, par * 64),
                               skip_group_check=True)
            pe_mark(mi); ms[f"pe_y2_{seg}_{gg}"] = cnt["pe"]
            A.wait_ge(s_pe, ms[f"pe_y1_{seg}_{gg}"])
            A.copy(out=yfm[:, 0:4, :], in_=Bk[2][:].rearrange("p (a n) -> p a n", a=4))
            A.wait_ge(s_pe, ms[f"pe_y2_{seg}_{gg}"])
            ai = A.copy(out=yfm[:, 4:8, :], in_=Bk[3][:].rearrange("p (a n) -> p a n", a=4))
            ac_mark(ai); ms[f"ac_ye_{seg}_{gg}"] = cnt["ac"]
            PE.wait_ge(s_dv, ms[f"dv_kln_{seg}_{gg}"])
            if gg >= 1:
                PE.wait_ge(s_dv, ms[f"dv_Ssb_{seg}_{gg-1}"])
            for h in range(16):
                par = h % 2
                mi = PE.matmul(Bk[7][par * 64:par * 64 + 64,
                                    (h // 2) * 64:(h // 2) * 64 + 64],
                               lhsT=kln_tm[:, h * 64:(h + 1) * 64],
                               rhs=v_tm[:, h * 64:(h + 1) * 64],
                               start=(gg == 0 and h < 2), stop=True,
                               tile_position=(0, par * 64), skip_group_check=True)
            pe_mark(mi); ms[f"pe_upd_{seg}_{gg}"] = cnt["pe"]
            V.wait_ge(s_pe, ms[f"pe_upd_{seg}_{gg}"])
            vo("Ssb", ["Scrm", "Scr"], lambda: V.tensor_tensor(
                out=S_sb[:], in0=Bk[7][:], in1=S_cr[:], op=ALU.add))
            dv_inc(); ms[f"dv_Ssb_{seg}_{gg}"] = cnt["dv"]

        def a_wo(gg):
            PE.wait_ge(s_ac, ms[f"ac_ye_{seg}_{gg}"])
            PE.wait_ge(s_dv, ms[f"dv_mask2_{seg}_{gg}"])
            for kk in range(8):
                for n2 in range(2):
                    mi = PE.matmul(Bk[4 + n2][:, :], lhsT=yfm[:, kk, :],
                                   rhs=wo[:, kk, n2 * 512:n2 * 512 + 512],
                                   start=(kk == 0), stop=(kk == 7))
            pe_mark(mi); ms[f"pe_wo_{seg}_{gg}"] = cnt["pe"]

        def a_ln2(gg):
            r = gg % 4
            xb = x_tm[:, gg % 2, :]
            V.wait_ge(s_pe, ms[f"pe_wo_{seg}_{gg}"])
            vo("bnsA", [], lambda: V.bn_stats(out=bns[:, 0:6], in_=Bk[4][:]))
            vo("bnsB", [], lambda: V.bn_stats(out=bns[:, 6:12], in_=Bk[5][:]))
            vo("mv2", [], lambda: V.bn_aggr(out=mv12[:, r, 2:4], in_=bns[:]))
            dv_inc(); ms[f"dv_ln2s_{seg}_{gg}"] = cnt["dv"]
            A.wait_ge(s_dv, ms[f"dv_ln2s_{seg}_{gg}"])
            ai = A.activation(out=scal[:, r, SL_SD2:SL_SD2 + 1], in_=mv12[:, r, 3:4],
                              func=AF.Sqrt, bias=epsv[:])
            ac_mark(ai); ms[f"ac_sd2_{seg}_{gg}"] = cnt["ac"]
            V.wait_ge(s_ac, ms[f"ac_sd2_{seg}_{gg}"])
            vo("r2", [], lambda: V.reciprocal(
                out=scal[:, r, SL_R2:SL_R2 + 1], in_=scal[:, r, SL_SD2:SL_SD2 + 1]))
            vo("b2s", [], lambda: V.tensor_scalar(
                out=scal[:, r, SL_B2:SL_B2 + 1], in0=mv12[:, r, 2:3],
                scalar1=scal[:, r, SL_R2:SL_R2 + 1], scalar2=-1.0,
                op0=ALU.mult, op1=ALU.mult))
            dv_inc(); ms[f"dv_ln2r_{seg}_{gg}"] = cnt["dv"]
            A.wait_ge(s_dv, ms[f"dv_ln2r_{seg}_{gg}"])
            A.activation(out=tmpa[:, 0:512], in_=Bk[4][:], func=AF.Identity,
                         scale=scal[:, r, SL_R2:SL_R2 + 1],
                         bias=scal[:, r, SL_B2:SL_B2 + 1])
            ai = A.activation(out=tmpa[:, 512:1024], in_=Bk[5][:], func=AF.Identity,
                              scale=scal[:, r, SL_R2:SL_R2 + 1],
                              bias=scal[:, r, SL_B2:SL_B2 + 1])
            ac_mark(ai); ms[f"ac_ln2a_{seg}_{gg}"] = cnt["ac"]

        def a_x2w(gg):
            xb = x_tm[:, gg % 2, :]
            V.wait_ge(s_ac, ms[f"ac_ln2a_{seg}_{gg}"])
            if seg > 0:
                wait_gd(V, ms[f"gd_out_{seg-1}_{gg}"])
            vo("x2w", [], lambda: V.tensor_tensor(
                out=x2[:, gg, :], in0=tmpa[:], in1=xb, op=ALU.add))
            dv_inc(); ms[f"dv_x2_{seg}_{gg}"] = cnt["dv"]

        a_load(0)
        if G > 1:
            a_load(1)
        a_front(0)
        a_front_tp(0)
        a_qk(0)
        a_v(0)
        a_kchain(0)
        a_kapp_half(0, 1)
        a_vevac(0)
        for gg in range(G):
            a_sc(gg)
            if gg + 1 < G:
                a_front(gg + 1)
                a_front_tp(gg + 1)
            a_y(gg)
            if gg + 1 < G:
                a_qk(gg + 1)
            a_wo(gg)
            if gg + 1 < G:
                a_v(gg + 1)
                a_kchain(gg + 1)
            a_ln2(gg)
            if gg + 1 < G:
                a_kapp_half(gg + 1, 1)
                a_vevac(gg + 1)
            a_x2w(gg)
            if gg + 2 < G:
                a_load(gg + 2)

        V.wait_ge(s_pe, ms[f"pe_upd_{seg}_{G-1}"])
        vo("Scr", [], lambda: V.tensor_tensor(
            out=S_cr[:], in0=Bk[7][:], in1=S_cr[:], op=ALU.add))
        dv_inc(); ms[f"dv_Scr_{seg}"] = cnt["dv"]
        ms[f"pe_attn_done_{seg}"] = cnt["pe"]

        GP.wait_ge(s_pe, ms[f"pe_attn_done_{seg}"])
        GP.wait_ge(s_dv, ms[f"dv_x2_{seg}_{G-1}"])
        if seg == 0:
            for i_ in range(8):
                GP.wait_ge(s_g[i_], gslot[i_])
        w1q, w2q = open_weights("ffn")
        ms[f"ffnw_{seg}"] = [gd_dma(w1q, w1q_h), gd_dma(w2q, w2q_h)]

        def f_front(gg):
            r = gg % 4
            sl = gg % 2
            h3 = h3r[:, sl, :]
            h4 = h4r[:, sl, :]
            fw = fwk[:, sl, :]
            if gg == 0:
                V.wait_ge(s_dv, ms[f"dv_Scr_{seg}"])
            vo("bnsA", [], lambda: V.bn_stats(out=bns[:, 0:6], in_=x2[:, gg, 0:512]))
            vo("bnsB", [], lambda: V.bn_stats(out=bns[:, 6:12], in_=x2[:, gg, 512:1024]))
            vo("mv3", [], lambda: V.bn_aggr(out=mvf[:, r, 0:2], in_=bns[:]))
            dv_inc(); ms[f"dv_ln3s_{seg}_{gg}"] = cnt["dv"]
            A.wait_ge(s_dv, ms[f"dv_ln3s_{seg}_{gg}"])
            ai = A.activation(out=scal[:, r, SL_T0:SL_T0 + 1], in_=mvf[:, r, 1:2],
                              func=AF.Sqrt, bias=epsv[:])
            ac_mark(ai); ms[f"ac_sd3_{seg}_{gg}"] = cnt["ac"]
            V.wait_ge(s_ac, ms[f"ac_sd3_{seg}_{gg}"])
            vo("r3", [], lambda: V.reciprocal(
                out=scal[:, r, SL_T1:SL_T1 + 1], in_=scal[:, r, SL_T0:SL_T0 + 1]))
            vo("b3s", [], lambda: V.tensor_scalar(
                out=scal[:, r, SL_T2:SL_T2 + 1], in0=mvf[:, r, 0:1],
                scalar1=scal[:, r, SL_T1:SL_T1 + 1], scalar2=-1.0,
                op0=ALU.mult, op1=ALU.mult))
            dv_inc(); ms[f"dv_ln3r_{seg}_{gg}"] = cnt["dv"]
            A.wait_ge(s_dv, ms[f"dv_ln3r_{seg}_{gg}"])
            ai = A.activation(out=h3, in_=x2[:, gg, :], func=AF.Identity,
                              scale=scal[:, r, SL_T1:SL_T1 + 1],
                              bias=scal[:, r, SL_T2:SL_T2 + 1])
            ac_mark(ai); ms[f"ac_h3_{seg}_{gg}"] = cnt["ac"]
            V.wait_ge(s_ac, ms[f"ac_h3_{seg}_{gg}"])
            vo("bnsA", [], lambda: V.bn_stats(out=bns[:, 0:6], in_=h3[:, 0:512]))
            vo("bnsB", [], lambda: V.bn_stats(out=bns[:, 6:12], in_=h3[:, 512:1024]))
            vo("mv4", [], lambda: V.bn_aggr(out=mvf[:, r, 2:4], in_=bns[:]))
            dv_inc(); ms[f"dv_ln4s_{seg}_{gg}"] = cnt["dv"]
            A.wait_ge(s_dv, ms[f"dv_ln4s_{seg}_{gg}"])
            ai = A.activation(out=scal[:, r, SL_T3:SL_T3 + 1], in_=mvf[:, r, 3:4],
                              func=AF.Sqrt, bias=epsv[:])
            ac_mark(ai); ms[f"ac_sd4_{seg}_{gg}"] = cnt["ac"]
            V.wait_ge(s_ac, ms[f"ac_sd4_{seg}_{gg}"])
            vo("r4", [], lambda: V.reciprocal(
                out=scal[:, r, SL_QM1:SL_QM1 + 1], in_=scal[:, r, SL_T3:SL_T3 + 1]))
            vo("tg", ["r4"], lambda: V.tensor_scalar(
                out=fw, in0=h3, scalar1=mvf[:, r, 2:3],
                scalar2=scal[:, r, SL_QM1:SL_QM1 + 1],
                op0=ALU.subtract, op1=ALU.mult))
            vo("h4a", ["tg"], lambda: V.tensor_tensor(
                out=h4, in0=fw, in1=gb[:], op=ALU.mult))
            vo("h4b", ["h4a"], lambda: V.tensor_tensor(
                out=h4, in0=h4, in1=bbt[:], op=ALU.add))
            vo("qm1", ["h4b"], lambda: V.tensor_reduce(
                out=scal[:, r, SL_QM1:SL_QM1 + 1], in_=h4, axis=AX.X, op=ALU.max,
                apply_absolute_value=True))
            vo("qm1c", ["qm1"], lambda: V.tensor_scalar(
                out=scal[:, r, SL_T0:SL_T0 + 1], in0=scal[:, r, SL_QM1:SL_QM1 + 1],
                scalar1=EPS, scalar2=None, op0=ALU.max))
            vo("dq1", ["qm1c"], lambda: V.tensor_scalar(
                out=scal[:, r, SL_DQ1:SL_DQ1 + 1], in0=scal[:, r, SL_T0:SL_T0 + 1],
                scalar1=swc[:, 1:2], scalar2=None, op0=ALU.mult))
            vo("iq1", ["qm1c"], lambda: V.reciprocal(
                out=scal[:, r, SL_T1:SL_T1 + 1], in_=scal[:, r, SL_T0:SL_T0 + 1]))
            vo("s1", ["iq1"], lambda: V.tensor_scalar(
                out=scal[:, r, SL_S1:SL_S1 + 1], in0=scal[:, r, SL_T1:SL_T1 + 1],
                scalar1=127.0, scalar2=None, op0=ALU.mult))
            vo("ss1", ["s1"], lambda: V.tensor_scalar(
                out=scal[:, r, SL_SS1:SL_SS1 + 1], in0=scal[:, r, SL_S1:SL_S1 + 1],
                scalar1=swc[:, 0:1], scalar2=None, op0=ALU.mult))
            vo("y1", ["s1"], lambda: V.tensor_scalar(
                out=fw, in0=h4, scalar1=scal[:, r, SL_S1:SL_S1 + 1],
                scalar2=mw[:], op0=ALU.mult, op1=ALU.add))
            if gg >= 2:
                V.wait_ge(s_tp, ms[f"tp_xq1_{seg}_{gg-2}"])
            vo("xq1", ["y1"], lambda: V.tensor_scalar(
                out=xq1_tm[:, sl, :], in0=fw, scalar1=mw[:], scalar2=None,
                op0=ALU.subtract))
            dv_inc(); ms[f"dv_xq1_{seg}_{gg}"] = cnt["dv"]

        def f_front_tp(gg):
            r = gg % 4
            sl = gg % 2
            PE.wait_ge(s_dv, ms[f"dv_xq1_{seg}_{gg}"])
            if gg >= 2:
                PE.wait_ge(s_ac, ms[f"ac_s1row_{seg}_{gg-2}"])
            mi = PE.transpose(Bk[6][0:1, sl * 128:sl * 128 + 128],
                              in_=scal[:, r, SL_SS1:SL_SS1 + 1], identity=identf[:])
            pe_mark(mi); ms[f"pe_s1t_{seg}_{gg}"] = cnt["pe"]
            A.wait_ge(s_pe, ms[f"pe_s1t_{seg}_{gg}"])
            ai = A.copy(out=s1rowr[:, sl, :], in_=Bk[6][0:1, sl * 128:sl * 128 + 128])
            ac_mark(ai); ms[f"ac_s1row_{seg}_{gg}"] = cnt["ac"]
            SY.wait_ge(s_dv, ms[f"dv_xq1_{seg}_{gg}"])
            if gg >= 2:
                SY.wait_ge(s_pe, ms[f"pe_f1_{seg}_{gg-2}_3"])
            tp_dma(xq1_fm[:, sl, :, :], xq1_tm[:, sl, :])
            ms[f"tp_xq1_{seg}_{gg}"] = cnt["tp"]

        def f_f1(gg):
            r = gg % 4
            sl = gg % 2
            xf = xq1_fm[:, sl, :, :]
            if gg == 0:
                for h_ in ms[f"ffnw_{seg}"]:
                    wait_gd(PE, h_)
            PE.wait_ge(s_tp, ms[f"tp_xq1_{seg}_{gg}"])
            PE.wait_ge(s_ac, ms[f"ac_s1row_{seg}_{gg}"])
            for slab in range(4):
                bb0 = 0 if slab % 2 == 0 else 2
                key = (f"ac_gel_{seg}_{gg}_{slab-2}" if slab >= 2 else
                       (f"ac_gel_{seg}_{gg-1}_{slab+2}" if gg >= 1 else None))
                if key is not None:
                    PE.wait_ge(s_ac, ms[key])
                for n2 in range(2):
                    hs0 = slab * 1024 + n2 * 512
                    for kk in range(8):
                        PE.matmul(Bk[bb0 + n2][:, :], lhsT=xf[:, kk, :],
                                  rhs=w1q[:, kk, hs0:hs0 + 512],
                                  start=(kk == 0), stop=False)
                    mi = PE.matmul(Bk[bb0 + n2][:, :], lhsT=s1rowr[:, sl, :],
                                   rhs=b1r[:, hs0:hs0 + 512], start=False, stop=True)
                pe_mark(mi); ms[f"pe_f1_{seg}_{gg}_{slab}"] = cnt["pe"]
                A.wait_ge(s_pe, ms[f"pe_f1_{seg}_{gg}_{slab}"])
                if gg >= 1:
                    A.wait_ge(s_dv, ms[f"dv_xq2c_{seg}_{gg-1}_{slab}"])
                for n2 in range(2):
                    hs0 = slab * 1024 + n2 * 512
                    ai = A.activation(out=g_sb[:, hs0:hs0 + 512], in_=Bk[bb0 + n2][:],
                                      func=AF.Gelu, scale=scal[:, r, SL_DQ1:SL_DQ1 + 1])
                ac_mark(ai); ms[f"ac_gel_{seg}_{gg}_{slab}"] = cnt["ac"]
                V.wait_ge(s_ac, ms[f"ac_gel_{seg}_{gg}_{slab}"])
                vo(f"qp{slab}", [], lambda slab=slab: V.tensor_reduce(
                    out=khs[:, r, 60 + slab:61 + slab],
                    in_=g_sb[:, slab * 1024:(slab + 1) * 1024],
                    axis=AX.X, op=ALU.max, apply_absolute_value=True))

        def f_q2(gg):
            r = gg % 4
            vo("qm2", ["qp0", "qp1", "qp2", "qp3"], lambda: V.tensor_reduce(
                out=scal[:, r, SL_QM2:SL_QM2 + 1], in_=khs[:, r, 60:64],
                axis=AX.X, op=ALU.max))
            vo("qm2c", ["qm2"], lambda: V.tensor_scalar(
                out=scal[:, r, SL_T2:SL_T2 + 1], in0=scal[:, r, SL_QM2:SL_QM2 + 1],
                scalar1=EPS, scalar2=None, op0=ALU.max))
            vo("dq2", ["qm2c"], lambda: V.tensor_scalar(
                out=scal[:, r, SL_DQ2:SL_DQ2 + 1], in0=scal[:, r, SL_T2:SL_T2 + 1],
                scalar1=swc[:, 3:4], scalar2=None, op0=ALU.mult))
            vo("iq2", ["qm2c"], lambda: V.reciprocal(
                out=scal[:, r, SL_T3:SL_T3 + 1], in_=scal[:, r, SL_T2:SL_T2 + 1]))
            vo("s2", ["iq2"], lambda: V.tensor_scalar(
                out=scal[:, r, SL_S2:SL_S2 + 1], in0=scal[:, r, SL_T3:SL_T3 + 1],
                scalar1=127.0, scalar2=None, op0=ALU.mult))
            vo("ss2", ["s2"], lambda: V.tensor_scalar(
                out=scal[:, r, SL_SS2:SL_SS2 + 1], in0=scal[:, r, SL_S2:SL_S2 + 1],
                scalar1=swc[:, 2:3], scalar2=None, op0=ALU.mult))
            dv_inc(); ms[f"dv_s2_{seg}_{gg}"] = cnt["dv"]
            for hh in range(4):
                if gg >= 1:
                    V.wait_ge(s_tp, ms[f"tp_xq2_{seg}_{gg-1}_{hh // 2}"])
                vo(f"y2{hh}", ["s2"], lambda hh=hh: V.tensor_scalar(
                    out=y2, in0=g_sb[:, hh * 1024:(hh + 1) * 1024],
                    scalar1=scal[:, r, SL_S2:SL_S2 + 1], scalar2=mw[:],
                    op0=ALU.mult, op1=ALU.add))
                vo(f"xq2{hh}", [f"y2{hh}"], lambda hh=hh: V.tensor_scalar(
                    out=xq2_tm[:, hh * 1024:(hh + 1) * 1024], in0=y2,
                    scalar1=mw[:], scalar2=None, op0=ALU.subtract))
                dv_inc(); ms[f"dv_xq2c_{seg}_{gg}_{hh}"] = cnt["dv"]
                if hh % 2 == 1:
                    half = hh // 2
                    ms[f"dv_xq2b_{seg}_{gg}_{half}"] = cnt["dv"]
                    SY.wait_ge(s_dv, cnt["dv"])
                    if gg >= 1:
                        SY.wait_ge(s_pe, ms[f"pe_f2_{seg}_{gg-1}_{half}"])
                    tp_dma(xq2_fm[:, half * 16:half * 16 + 16, :],
                           xq2_tm[:, half * 2048:(half + 1) * 2048])
                    ms[f"tp_xq2_{seg}_{gg}_{half}"] = cnt["tp"]

        def f_f2(gg):
            r = gg % 4
            sl = gg % 2
            PE.wait_ge(s_dv, ms[f"dv_s2_{seg}_{gg}"])
            if gg >= 2:
                PE.wait_ge(s_ac, ms[f"ac_s2row_{seg}_{gg-2}"])
            mi = PE.transpose(Bk[6][0:1, 256 + sl * 128:256 + sl * 128 + 128],
                              in_=scal[:, r, SL_SS2:SL_SS2 + 1], identity=identf[:])
            pe_mark(mi); ms[f"pe_s2t_{seg}_{gg}"] = cnt["pe"]
            A.wait_ge(s_pe, ms[f"pe_s2t_{seg}_{gg}"])
            ai = A.copy(out=s2rowr[:, sl, :],
                        in_=Bk[6][0:1, 256 + sl * 128:256 + sl * 128 + 128])
            ac_mark(ai); ms[f"ac_s2row_{seg}_{gg}"] = cnt["ac"]
            for half in range(2):
                PE.wait_ge(s_tp, ms[f"tp_xq2_{seg}_{gg}_{half}"])
                if half == 0:
                    PE.wait_ge(s_ac, ms[f"ac_s2row_{seg}_{gg}"])
                    if gg >= 1:
                        PE.wait_ge(s_ac, ms[f"ac_z_{seg}_{gg-1}"])
                for n2 in range(2):
                    for kk in range(half * 16, half * 16 + 16):
                        mi = PE.matmul(Bk[4 + n2][:, :], lhsT=xq2_fm[:, kk, :],
                                       rhs=w2q[:, kk, n2 * 512:n2 * 512 + 512],
                                       start=(kk == 0), stop=False)
                    if half == 1:
                        mi = PE.matmul(Bk[4 + n2][:, :], lhsT=s2rowr[:, sl, :],
                                       rhs=b2r[:, n2 * 512:n2 * 512 + 512],
                                       start=False, stop=True)
                pe_mark(mi); ms[f"pe_f2_{seg}_{gg}_{half}"] = cnt["pe"]
            ms[f"pe_f2_{seg}_{gg}"] = ms[f"pe_f2_{seg}_{gg}_1"]
            zt = ztr[:, sl, :]
            A.wait_ge(s_pe, ms[f"pe_f2_{seg}_{gg}"])
            if gg >= 2:
                A.wait_ge(s_dv, ms[f"dv_out_{seg}_{gg-2}"])
            A.activation(out=zt[:, 0:512], in_=Bk[4][:], func=AF.Identity,
                         scale=scal[:, r, SL_DQ2:SL_DQ2 + 1])
            ai = A.activation(out=zt[:, 512:1024], in_=Bk[5][:], func=AF.Identity,
                              scale=scal[:, r, SL_DQ2:SL_DQ2 + 1])
            ac_mark(ai); ms[f"ac_z_{seg}_{gg}"] = cnt["ac"]

        def f_tail(gg):
            r = gg % 4
            zt = ztr[:, gg % 2, :]
            znt = zt
            V.wait_ge(s_ac, ms[f"ac_z_{seg}_{gg}"])
            vo("bnsA", [], lambda: V.bn_stats(out=bns[:, 0:6], in_=zt[:, 0:512]))
            vo("bnsB", [], lambda: V.bn_stats(out=bns[:, 6:12], in_=zt[:, 512:1024]))
            vo("mv5", [], lambda: V.bn_aggr(out=mv5[:, r, 0:2], in_=bns[:]))
            dv_inc(); ms[f"dv_ln5s_{seg}_{gg}"] = cnt["dv"]
            A.wait_ge(s_dv, ms[f"dv_ln5s_{seg}_{gg}"])
            ai = A.activation(out=scal[:, r, SL_SD5:SL_SD5 + 1], in_=mv5[:, r, 1:2],
                              func=AF.Sqrt, bias=epsv[:])
            ac_mark(ai); ms[f"ac_sd5_{seg}_{gg}"] = cnt["ac"]
            V.wait_ge(s_ac, ms[f"ac_sd5_{seg}_{gg}"])
            vo("r5", [], lambda: V.reciprocal(
                out=scal[:, r, SL_R5:SL_R5 + 1], in_=scal[:, r, SL_SD5:SL_SD5 + 1]))
            vo("b5s", [], lambda: V.tensor_scalar(
                out=scal[:, r, SL_B5:SL_B5 + 1], in0=mv5[:, r, 0:1],
                scalar1=scal[:, r, SL_R5:SL_R5 + 1], scalar2=-1.0,
                op0=ALU.mult, op1=ALU.mult))
            dv_inc(); ms[f"dv_ln5r_{seg}_{gg}"] = cnt["dv"]
            A.wait_ge(s_dv, ms[f"dv_ln5r_{seg}_{gg}"])
            ai = A.activation(out=znt, in_=zt, func=AF.Identity,
                              scale=scal[:, r, SL_R5:SL_R5 + 1],
                              bias=scal[:, r, SL_B5:SL_B5 + 1])
            ac_mark(ai); ms[f"ac_zn_{seg}_{gg}"] = cnt["ac"]
            V.wait_ge(s_ac, ms[f"ac_zn_{seg}_{gg}"])
            vo("outw", [], lambda: V.tensor_tensor(
                out=x2[:, gg, :], in0=znt, in1=x2[:, gg, :], op=ALU.add))
            dv_inc(); ms[f"dv_out_{seg}_{gg}"] = cnt["dv"]
            GP.wait_ge(s_dv, ms[f"dv_out_{seg}_{gg}"])
            ms[f"gd_out_{seg}_{gg}"] = gd_dma(
                out_h[(g0 + gg) * 128:(g0 + gg) * 128 + 128, :], x2[:, gg, :])

        f_front(0)
        f_front_tp(0)
        if G > 1:
            f_front(1)
        f_f1(0)
        for gg in range(G):
            if gg + 1 < G:
                f_front_tp(gg + 1)
            f_q2(gg)
            if gg + 2 < G:
                f_front(gg + 2)
            if gg + 1 < G:
                f_f1(gg + 1)
            f_f2(gg)
            if gg >= 1:
                f_tail(gg - 1)
        f_tail(G - 1)
        ms[f"pe_ffn_done_{seg}"] = cnt["pe"]

    for i in range(8):
        SY.wait_ge(s_g[i], gslot[i])
    for i_ in (1, 0):
        if _wcm[i_] is not None:
            _wcm[i_].__exit__(None, None, None)
            _wcm[i_] = None
    es.close()
    return nc


def kernel(L0_attn_x, L0_attn_W_qkv, L0_attn_W_o, L0_ffn_w1, L0_ffn_b1,
           L0_ffn_w2, L0_ffn_b2, L0_ffn_gamma, L0_ffn_beta):
    x = np.ascontiguousarray(np.asarray(L0_attn_x, dtype=np.float32))
    B, L, Dm = x.shape
    nc = build(NT=L)
    common = {
        "w_qkv": np.ascontiguousarray(L0_attn_W_qkv, dtype=np.float32),
        "w_o": np.ascontiguousarray(L0_attn_W_o, dtype=np.float32),
        "w1": np.ascontiguousarray(L0_ffn_w1, dtype=np.float32),
        "b1": np.ascontiguousarray(L0_ffn_b1, dtype=np.float32).reshape(1, -1),
        "w2": np.ascontiguousarray(L0_ffn_w2, dtype=np.float32),
        "b2": np.ascontiguousarray(L0_ffn_b2, dtype=np.float32).reshape(1, -1),
        "gamma": np.ascontiguousarray(L0_ffn_gamma, dtype=np.float32).reshape(1, -1),
        "beta": np.ascontiguousarray(L0_ffn_beta, dtype=np.float32).reshape(1, -1),
    }
    in_maps = [{"x": np.ascontiguousarray(x[b]), **common} for b in range(B)]
    res = run_bass_kernel_spmd(nc, in_maps, core_ids=list(range(B)))
    return np.stack([res.results[b]["out"] for b in range(B)], axis=0)
